# version 08
# baseline (speedup 1.0000x reference)
import numpy as np
import ml_dtypes
from contextlib import ExitStack
import concourse.bass as bass
import concourse.mybir as mybir
from concourse.bass_utils import run_bass_kernel_spmd

F32 = mybir.dt.float32
BF16 = mybir.dt.bfloat16
AL = mybir.AluOpType
AF = mybir.ActivationFunctionType
BF = ml_dtypes.bfloat16

D = 2048
NH = 32
HP = 64
CH = 128
TOK = 1024
NT = 8
DK = 16
II = 8192
NIG = 16
EPS = 1e-5
NEG = -1e30
NUDGE = 2.0 ** -20


class Sem:
    def __init__(self, nc, name):
        self.h = nc.alloc_semaphore(name)
        self.n = 0


class K:
    ENGS = ("sync", "tensor", "vector", "scalar", "gpsimd")

    def __init__(self, nc):
        self.nc = nc
        self.ops = {e: [] for e in self.ENGS}

    def do(self, eng, fname, *args, inc=None, **kwargs):
        self.ops[eng].append(("op", fname, args, kwargs, inc))
        if inc is not None:
            for s, kk in inc:
                s.n += kk

    def wait(self, eng, sem, val=None):
        v = sem.n if val is None else val
        assert v <= sem.n, "wait target not yet produced at script time"
        self.ops[eng].append(("wait", sem, v))

    def _replay(self, eng_name, eng):
        for rec in self.ops[eng_name]:
            if rec[0] == "wait":
                _, sem, val = rec
                if val > 0:
                    eng.wait_ge(sem.h, val)
            else:
                _, fname, args, kwargs, inc = rec
                instr = getattr(eng, fname)(*args, **kwargs)
                if inc is not None:
                    for s, kk in inc:
                        instr = instr.then_inc(s.h, kk)

    def build(self):
        with self.nc.Block() as block:
            block.sync(lambda e: self._replay("sync", e))
            block.tensor(lambda e: self._replay("tensor", e))
            block.vector(lambda e: self._replay("vector", e))
            block.scalar(lambda e: self._replay("scalar", e))
            block.gpsimd(lambda e: self._replay("gpsimd", e))


def build_program(stop_after=None, detect_races=True, dbg=(), chunk_steps=99, d_parts=99):
    nc = bass.Bass("TRN2", target_bir_lowering=False,
                   dynamic_dma_scratch_size=16384,
                   detect_race_conditions=detect_races)

    def din(name, shape, dt=F32):
        return nc.dram_tensor(name, list(shape), dt, kind="ExternalInput").ap()

    x_hbm = din("x_own", [TOK, D])
    wg_hbm = din("wg", [D, II])
    wu_hbm = din("wu", [D, II])
    wd_hbm = din("wd_perm", [II, D])
    owt_hbm = din("ow_t", [D, D])
    bcdt_hbm = din("bcdt_wt", [D, 160])
    bcdtb_hbm = din("bcdt_b", [1, 160])
    outb_hbm = din("out_b", [1, D])
    alog_hbm = din("a_log", [1, NH])
    dvec_hbm = din("dvec", [1, NH])
    ident_hbm = din("ident128", [128, 128])
    m0_hbm = din("m0", [128, 128])
    ones_hbm = din("ones_row", [1, 128])
    csel_hbm = din("csel", [1, 4])
    out_hbm = nc.dram_tensor("out", [TOK, D], F32, kind="ExternalOutput").ap()

    cc_src = nc.dram_tensor("cc_src", [2049, 64], F32, kind="Internal")
    cc_dst = nc.dram_tensor("cc_dst", [4 * 2049, 64], F32, kind="Internal")

    ctx = ExitStack()
    sb = lambda name, shape, dt: ctx.enter_context(nc.sbuf_tensor(name, shape, dt))

    xnt = sb("xnt", [128, DK * TOK], BF16)
    ybuf = sb("ybuf", [128, NT * D], BF16)
    xstage = sb("xstage", [128, 2 * D], F32)
    identb = sb("identb", [128, 128], BF16)
    onesf = sb("onesf", [1, 128], F32)
    onesb = sb("onesb", [1, 128], BF16)

    k = K(nc)
    s_cld = Sem(nc, "cld")
    s_cldg = Sem(nc, "cldg")
    s_xld = [Sem(nc, "xld0"), Sem(nc, "xld1")]
    s_lnv = Sem(nc, "lnv")
    s_lna = Sem(nc, "lna")
    s_tps = Sem(nc, "tps")
    s_tev = Sem(nc, "tev")
    s_mm = Sem(nc, "smm")
    s_ve = Sem(nc, "sve")
    s_ac = Sem(nc, "sac")
    s_gp = Sem(nc, "sgp")
    s_rg = [Sem(nc, "srg0"), Sem(nc, "srg1")]
    s_cc = Sem(nc, "scc")
    s_wgl = Sem(nc, "wgl")
    s_wul = Sem(nc, "wul")
    s_wd = [Sem(nc, "swd%d" % i) for i in range(4)]
    s_ow = [Sem(nc, "sow%d" % i) for i in range(4)]
    s_st = [Sem(nc, "sst0"), Sem(nc, "sst1")]
    s_hev = Sem(nc, "hev")
    s_spr = Sem(nc, "spr")
    s_gev = Sem(nc, "gev")
    s_xld2 = Sem(nc, "xld2")
    s_owt = Sem(nc, "owt")
    s_sev = Sem(nc, "sev")

    def A(t, c0, w):
        return t[:, c0:c0 + w]

    ab = ExitStack()
    sba = lambda name, shape, dt: ab.enter_context(nc.sbuf_tensor(name, shape, dt))
    xn = sba("xn", [128, NT * D], BF16)
    identf = sba("identf", [128, 128], F32)
    m0c = sba("m0c", [128, 128], F32)
    aux = sba("aux", [128, 128], F32)
    aux2 = sba("aux2", [128, 512], F32)
    lncol = sba("lncol", [128, 48], F32)
    stat6 = sba("stat6", [128, 48], F32)
    rows = sba("rows", [1, 256], F32)
    bcdtw = sba("bcdtw", [128, DK * 160], BF16)
    bcdtb = sba("bcdtb", [1, 160], BF16)
    btok = sba("btok", [128, NT * HP], BF16)
    ctok = sba("ctok", [128, NT * HP], BF16)
    bT = sba("bT", [64, NT * CH], BF16)
    cT = sba("cT", [64, NT * CH], BF16)
    dtf = sba("dtf", [128, NT * NH], F32)
    dtds = sba("dtds", [128, NT * NH], F32)
    acs = sba("acs", [128, NT * NH], F32)
    nacs = sba("nacs", [128, NT * NH], F32)
    sdo = sba("sdo", [128, NT * NH], F32)
    acsT = sba("acsT", [32, CH], F32)
    acsT_hi = sba("acsT_hi", [32, NT * CH], BF16)
    acsT_lo = sba("acsT_lo", [32, NT * CH], BF16)
    onest = sba("onest", [32, CH], F32)
    alast = sba("alast", [32, NT], F32)
    pincl = sba("pincl", [32, NT], F32)
    pexcl = sba("pexcl", [32, NT], F32)
    flrow = sba("flrow", [1, 544], F32)
    csrow_hi = sba("csrow_hi", [1, 2048], BF16)
    csrow_lo = sba("csrow_lo", [1, 2048], BF16)
    sct = sba("sct", [128, NT * CH], BF16)
    spscr = sba("spscr", [128, 96], F32)
    datile = sba("datile", [128, NH], F32)
    rloc = sba("rloc", [64, NT * NH * HP], BF16)
    xdtp = sba("xdtp", [128, 2 * CH], BF16)
    xdtd = sba("xdtd", [128, 4 * HP], BF16)
    mt = sba("mt", [128, 4 * CH], BF16)
    t1f = sba("t1f", [128, 2 * CH], F32)
    t2f = sba("t2f", [128, 2 * CH], F32)
    yscr = sba("yscr", [128, 2 * HP], F32)
    rg = sba("rg", [64, 8 * HP], F32)
    prows = sba("prows", [1, 192], F32)
    prep = sba("prep", [128, 192], F32)
    initb = sba("initb", [64, NH * HP], F32)
    i2scr = sba("i2scr", [64, 2 * HP], F32)
    prevt = sba("prevt", [64, 16 * HP], BF16)

    dbg_map = dict(flrow=flrow, pincl=pincl, pexcl=pexcl, alast=alast,
               xn=xn, xnt=xnt, btok=btok, ctok=ctok, bT=bT, cT=cT,
                   dtf=dtf, dtds=dtds, acs=acs, nacs=nacs, sdo=sdo,
                   acsT_hi=acsT_hi, sct=sct, rloc=rloc,
                   aux=aux, aux2=aux2, prep=prep, initb=initb, ybuf=ybuf,
                   prevt=prevt, xstage=xstage)

    def emit_dumps():
        for name in dbg:
            if name in ("cc_src", "cc_dst"):
                t = {"cc_src": cc_src, "cc_dst": cc_dst}[name]
                dout = nc.dram_tensor("dbg_" + name, list(t.shape), F32,
                                      kind="ExternalOutput").ap()
                for s in (s_ve, s_ac, s_mm, s_gp):
                    k.wait("gpsimd", s)
                k.do("gpsimd", "dma_start", out=dout[:, :], in_=t.ap()[:, :],
                     inc=[(s_gp, 16)])
                k.wait("gpsimd", s_gp)
                continue
            t = dbg_map[name]
            shp = list(t.shape)
            dout = nc.dram_tensor("dbg_" + name, shp, F32,
                                  kind="ExternalOutput").ap()
            for s in (s_ve, s_ac, s_mm, s_tev, s_lna):
                k.wait("gpsimd", s)
            nrow = shp[0]
            nspl = max(1, (nrow * max(1, shp[1] // 8192)) // 512)
            k.do("gpsimd", "dma_start", out=dout[:, :], in_=t[:, :],
                 inc=[(s_gp, 16)])
            k.wait("gpsimd", s_gp)

    pa = ExitStack()
    ps_sa = pa.enter_context(nc.psum_tensor("ps_sa", [128, 512], F32))
    ps_sb = pa.enter_context(nc.psum_tensor("ps_sb", [128, 512], F32))
    ps_sd = pa.enter_context(nc.psum_tensor("ps_sd", [128, 512], F32))
    pa2 = ExitStack()
    ps_t1 = pa2.enter_context(nc.psum_tensor("ps_t1", [128, 128], BF16))
    ps_t2 = pa2.enter_context(nc.psum_tensor("ps_t2", [128, 128], BF16))
    ps_fx = pa2.enter_context(nc.psum_tensor("ps_fx", [128, 16], F32))

    k.do("gpsimd", "dma_start", out=identb[:], in_=ident_hbm[:, :], inc=[(s_cldg, 16)])
    k.do("sync", "dma_start", out=m0c[:], in_=m0_hbm[:, :], inc=[(s_cld, 16)])
    k.do("sync", "dma_start", out=identf[:], in_=ident_hbm[:, :], inc=[(s_cld, 16)])
    k.do("sync", "dma_start", out=onesf[:], in_=ones_hbm[:, :], inc=[(s_cld, 16)])
    k.do("sync", "dma_start", out=rows[0:1, 0:32], in_=alog_hbm[:, :], inc=[(s_cld, 16)])
    k.do("sync", "dma_start", out=rows[0:1, 32:64], in_=dvec_hbm[:, :], inc=[(s_cld, 16)])
    k.do("sync", "dma_start", out=rows[0:1, 64:68], in_=csel_hbm[:, :], inc=[(s_cld, 16)])
    k.do("gpsimd", "dma_start", out=onesb[:], in_=ones_hbm[:, :], inc=[(s_cldg, 16)])
    for qq in range(4):
        k.do("gpsimd", "dma_start",
             out=bcdtw[:].rearrange("p (k c) -> p k c", c=160)[:, qq * 4:(qq + 1) * 4, :],
             in_=bcdt_hbm[:, :].rearrange("(k p) c -> p k c", p=128)[:, qq * 4:(qq + 1) * 4, :],
             inc=[(s_cldg, 16)])
    k.do("gpsimd", "dma_start", out=bcdtb[:], in_=bcdtb_hbm[:, :], inc=[(s_cldg, 16)])
    CLD = s_cld.n
    CLDG = s_cldg.n

    k.wait("scalar", s_cld, CLD)
    k.wait("scalar", s_cldg, CLDG)
    k.do("scalar", "activation", rows[0:1, 128:160], rows[0:1, 0:32], AF.Exp,
         inc=[(s_ac, 1)])
    k.wait("tensor", s_ac)
    k.wait("tensor", s_cld, CLD)
    k.wait("tensor", s_cldg, CLDG)
    k.do("tensor", "matmul", ps_sa[:, 0:32], onesf[:], rows[0:1, 128:160],
         start=True, stop=True)
    k.do("tensor", "matmul", ps_sa[:, 32:64], onesf[:], rows[0:1, 32:64],
         start=True, stop=True)
    k.do("tensor", "matmul", ps_sa[:, 64:68], onesf[:], rows[0:1, 64:68],
         start=True, stop=True, inc=[(s_mm, 1)])
    k.wait("vector", s_mm)
    k.do("vector", "tensor_scalar", aux[:, 0:32], ps_sa[:, 0:32], -1.0, None, AL.mult)
    k.do("vector", "tensor_copy", aux[:, 32:68], ps_sa[:, 32:68])
    k.do("vector", "memset", onest[:], 1.0, inc=[(s_ve, 1)])
    VE_AUX = s_ve.n

    lna_tile = []
    for t in range(NT):
        sl = (t % 2) * D
        if t >= 2:
            k.wait("sync", s_lna, lna_tile[t - 2])
        k.do("sync", "dma_start", out=A(xstage, sl, D),
             in_=x_hbm[t * 128:(t + 1) * 128, :], inc=[(s_xld[t % 2], 16)])
        c = 6 * t
        st6 = stat6[:, (t % 2) * 24:(t % 2) * 24 + 24]
        k.wait("vector", s_xld[t % 2], 16 * (t // 2 + 1))
        for q in range(4):
            k.do("vector", "bn_stats", st6[:, q * 6:(q + 1) * 6],
                 A(xstage, sl + q * 512, 512),
                 inc=[(s_lnv, 1)] if q == 3 else None)
        k.wait("vector", s_lnv)
        k.do("vector", "bn_aggr", lncol[:, c:c + 2], st6, inc=[(s_lnv, 1)])
        k.wait("vector", s_lnv)
        k.do("vector", "tensor_scalar", lncol[:, c + 2:c + 3],
             lncol[:, c + 1:c + 2], float(EPS), None, AL.add, inc=[(s_lnv, 1)])
        k.wait("scalar", s_lnv)
        k.do("scalar", "activation", lncol[:, c + 3:c + 4], lncol[:, c + 2:c + 3],
             AF.Ln)
        k.do("scalar", "activation", lncol[:, c + 4:c + 5], lncol[:, c + 3:c + 4],
             AF.Exp, scale=-0.5, inc=[(s_lna, 1)])
        k.wait("vector", s_lna)
        k.do("vector", "scalar_tensor_tensor", lncol[:, c + 5:c + 6],
             lncol[:, c:c + 1], -1.0, lncol[:, c + 4:c + 5], AL.mult, AL.mult,
             inc=[(s_lnv, 1)])
        k.do("vector", "tensor_scalar", A(xn, t * D, D), A(xstage, sl, D),
             lncol[:, c + 4:c + 5], lncol[:, c + 5:c + 6], AL.mult, AL.add,
             inc=[(s_lna, 1)])
        lna_tile.append(s_lna.n)

    if stop_after == "A1":
        pa2.close(); pa.close(); ab.close(); ctx.close(); k.build()
        return nc

    k.wait("tensor", s_cld, CLD)
    k.wait("tensor", s_cldg, CLDG)
    nT = 0
    for t in range(NT):
        k.wait("tensor", s_lna, lna_tile[t])
        for dk in range(DK):
            pst = ps_t1 if nT % 2 == 0 else ps_t2
            if nT >= 2:
                k.wait("tensor", s_tev, nT - 1)
            k.do("tensor", "transpose", pst[:, :],
                 A(xn, t * D + dk * 128, 128), identb[:])
            k.do("tensor", "matmul", ps_fx[0:1, 0:1], onesb[0:1, 0:1],
                 onesb[0:1, 0:1], start=True, stop=True, inc=[(s_tps, 1)])
            k.wait("vector", s_tps)
            k.do("vector", "tensor_copy", A(xnt, dk * TOK + t * 128, 128),
                 pst[:, :], inc=[(s_tev, 1)])
            nT += 1

    if stop_after == "A2":
        pa2.close(); pa.close(); ab.close(); ctx.close(); k.build()
        return nc

    for t in range(NT):
        k.wait("tensor", s_tev, (t + 1) * DK)
        if t > 0:
            k.wait("tensor", s_ac)
            k.wait("tensor", s_ve)
        for dk in range(DK):
            k.do("tensor", "matmul", ps_sb[:, 0:160],
                 A(xnt, dk * TOK + t * 128, 128), A(bcdtw, dk * 160, 160),
                 start=(dk == 0), stop=False)
        k.do("tensor", "matmul", ps_sb[:, 0:160], onesb[:], bcdtb[:],
             start=False, stop=True, inc=[(s_mm, 1)])
        MM_PROJ = s_mm.n
        k.wait("vector", s_mm, MM_PROJ)
        k.do("vector", "tensor_copy", A(btok, t * HP, HP), ps_sb[:, 0:64])
        k.do("vector", "tensor_copy", A(ctok, t * HP, HP), ps_sb[:, 64:128],
             inc=[(s_ve, 1)])
        if chunk_steps < 2:
            continue
        k.wait("scalar", s_mm, MM_PROJ)
        k.wait("scalar", s_ve)
        k.do("scalar", "activation", spscr[:, 0:32], ps_sb[:, 128:160], AF.Abs)
        k.do("scalar", "activation", spscr[:, 32:64], spscr[:, 0:32], AF.Exp,
             scale=-1.0)
        k.do("scalar", "activation", spscr[:, 64:96], spscr[:, 32:64], AF.Ln,
             bias=1.0)
        k.do("scalar", "activation", spscr[:, 0:32], ps_sb[:, 128:160], AF.Relu,
             inc=[(s_ac, 1)])
        k.wait("vector", s_ac)
        k.do("vector", "tensor_add", A(dtf, t * NH, NH), spscr[:, 0:32],
             spscr[:, 64:96])
        k.wait("vector", s_ve, VE_AUX)
        k.do("vector", "tensor_mul", datile[:], A(dtf, t * NH, NH), aux[:, 0:32],
             inc=[(s_ve, 1)])
        if chunk_steps < 3:
            continue
        k.wait("tensor", s_ve)
        k.do("tensor", "transpose", ps_sa[0:32, 128:256],
             datile[:], identf[:])
        k.do("tensor", "matmul", ps_fx[0:1, 0:1], onesb[0:1, 0:1],
             onesb[0:1, 0:1], start=True, stop=True, inc=[(s_tps, 1)])
        k.wait("vector", s_tps)
        k.do("vector", "tensor_tensor_scan", acsT[:, :], onest[:],
             ps_sa[0:32, 128:256], 0.0, AL.mult, AL.add, inc=[(s_ve, 1)])
        k.wait("vector", s_ve)
        k.wait("tensor", s_ve)
        k.do("tensor", "transpose", ps_sa[:, 256:288], acsT[:, :],
             identf[0:32, 0:32])
        k.do("tensor", "matmul", ps_fx[0:1, 0:1], onesb[0:1, 0:1],
             onesb[0:1, 0:1], start=True, stop=True, inc=[(s_tps, 1)])
        k.wait("vector", s_tps)
        k.do("vector", "tensor_copy", A(acs, t * NH, NH), ps_sa[:, 256:288])
        k.do("vector", "tensor_scalar", A(nacs, t * NH, NH), ps_sa[:, 256:288],
             -1.0, None, AL.mult)
        k.do("vector", "tensor_copy", alast[:, t:t + 1], acsT[:, 127:128],
             inc=[(s_ve, 1)])
        if chunk_steps < 4:
            continue
        VE_ACS = s_ve.n
        k.wait("scalar", s_ve, VE_ACS)
        k.do("scalar", "activation", A(acsT_hi, t * CH, CH)[0:32, :], acsT[:, :],
             AF.Copy)
        k.do("scalar", "activation", A(sdo, t * NH, NH), A(acs, t * NH, NH),
             AF.Exp, inc=[(s_ac, 1)])
        k.wait("vector", s_ac)
        k.do("vector", "tensor_sub", A(acsT_lo, t * CH, CH)[0:32, :], acsT[:, :],
             A(acsT_hi, t * CH, CH)[0:32, :], inc=[(s_ve, 1)])
        if chunk_steps < 5:
            continue
        k.wait("gpsimd", s_ve, VE_ACS)
        k.wait("gpsimd", s_mm)
        k.do("gpsimd", "dma_start", out=flrow[0:1, 512:544],
             in_=alast[:, t:t + 1], inc=[(s_gp, 16)])
        k.wait("tensor", s_gp)
        k.wait("tensor", s_ve)
        k.do("tensor", "matmul", ps_sa[:, 288:320], onesf[:], flrow[0:1, 512:544],
             start=True, stop=True, inc=[(s_mm, 1)])
        k.wait("vector", s_mm)
        k.do("vector", "tensor_sub", t1f[:, 0:32], ps_sa[:, 288:320],
             A(acs, t * NH, NH), inc=[(s_ve, 1)])
        k.wait("scalar", s_ve)
        k.do("scalar", "activation", t2f[:, 0:32], t1f[:, 0:32], AF.Exp)
        k.do("scalar", "activation", t2f[:, 32:64], ps_sa[:, 288:320], AF.Exp,
             inc=[(s_ac, 1)])
        k.wait("vector", s_ac)
        k.do("vector", "tensor_mul", A(dtds, t * NH, NH), A(dtf, t * NH, NH),
             t2f[:, 0:32], inc=[(s_ve, 1)])
        VE_DTDS = s_ve.n

        if chunk_steps < 6:
            continue
        k.wait("tensor", s_ve, VE_DTDS)
        k.wait("tensor", s_tev)
        k.do("tensor", "transpose", ps_t1[0:64, :], A(btok, t * HP, HP),
             identb[:])
        k.do("tensor", "transpose", ps_t2[0:64, :], A(ctok, t * HP, HP),
             identb[:])
        k.do("tensor", "matmul", ps_fx[0:1, 0:1], onesb[0:1, 0:1],
             onesb[0:1, 0:1], start=True, stop=True, inc=[(s_tps, 1)])
        k.wait("vector", s_tps)
        k.do("vector", "tensor_copy", A(bT, t * CH, CH)[0:64, :], ps_t1[0:64, :])
        k.do("vector", "tensor_copy", A(cT, t * CH, CH)[0:64, :], ps_t2[0:64, :],
             inc=[(s_tev, 1)])
        k.wait("tensor", s_tev)
        k.do("tensor", "matmul", ps_sd[:, 0:128], A(bT, t * CH, CH)[0:64, :],
             A(cT, t * CH, CH)[0:64, :], start=True, stop=True, inc=[(s_mm, 1)])
        k.wait("vector", s_mm)
        k.do("vector", "tensor_copy", A(sct, t * CH, CH), ps_sd[:, 0:128],
             inc=[(s_ve, 1)])
        SCT_EV = s_ve.n

        if chunk_steps < 7:
            continue
        for hp2 in range(16):
            h0, h1 = 2 * hp2, 2 * hp2 + 1
            k.wait("vector", s_ve, VE_DTDS)
            k.do("vector", "tensor_scalar", A(xdtp, 0, HP),
                 A(xn, t * D + h0 * HP, HP), A(dtds, t * NH + h0, 1), None,
                 AL.mult)
            k.do("vector", "tensor_scalar", A(xdtp, HP, HP),
                 A(xn, t * D + h1 * HP, HP), A(dtds, t * NH + h1, 1), None,
                 AL.mult, inc=[(s_ve, 1)])
            k.wait("tensor", s_ve)
            pst, pc0 = (ps_sa, 320) if hp2 % 2 == 0 else (ps_sd, 256)
            k.do("tensor", "matmul", pst[0:64, pc0:pc0 + 128],
                 A(btok, t * HP, HP), xdtp[:, 0:128],
                 start=True, stop=True, inc=[(s_mm, 1)])
            k.wait("vector", s_mm)
            k.wait("vector", s_ac)
            for hh, hx in ((h0, 0), (h1, 1)):
                pcol = pc0 + hx * HP
                if t == 0:
                    k.do("vector", "tensor_copy",
                         A(rloc, (t * NH + hh) * HP, HP)[0:64, :],
                         pst[0:64, pcol:pcol + HP], inc=[(s_ve, 1)])
                else:
                    k.do("vector", "scalar_tensor_tensor",
                         A(rloc, (t * NH + hh) * HP, HP)[0:64, :],
                         A(rloc, ((t - 1) * NH + hh) * HP, HP)[0:64, :],
                         t2f[0:64, 32 + hh:33 + hh],
                         pst[0:64, pcol:pcol + HP], AL.mult, AL.add,
                         inc=[(s_ve, 1)])
    CHAIN_DONE = s_ve.n

    if stop_after == "A3":
        pa2.close(); pa.close(); ab.close(); ctx.close(); k.build()
        return nc

    k.do("vector", "tensor_tensor_scan", pincl[:, :], onest[:, 0:NT], alast[:, :],
         0.0, AL.mult, AL.add, inc=[(s_ve, 1)])
    k.wait("vector", s_ve)
    k.do("vector", "tensor_sub", pexcl[:, :], pincl[:, :], alast[:, :],
         inc=[(s_ve, 1)])
    k.wait("gpsimd", s_ve)
    k.do("gpsimd", "dma_start", out=flrow[0:1, 0:256], in_=alast[:, :],
         inc=[(s_gp, 16)])
    k.do("gpsimd", "dma_start", out=flrow[0:1, 256:512], in_=pexcl[:, :],
         inc=[(s_gp, 16)])
    k.wait("scalar", s_gp)
    k.wait("scalar", s_ve)
    k.do("scalar", "activation", flrow[0:1, 0:256], flrow[0:1, 0:256], AF.Exp)
    k.do("scalar", "activation", flrow[0:1, 256:512], flrow[0:1, 256:512], AF.Exp)
    k.do("scalar", "activation", spscr[0:32, 0:1], pincl[:, 7:8], AF.Exp,
         inc=[(s_ac, 1)])
    k.wait("tensor", s_ac)
    k.wait("tensor", s_ve)
    k.do("tensor", "matmul", ps_sa[:, 0:256], onesf[:], flrow[0:1, 0:256],
         start=True, stop=True)
    k.do("tensor", "matmul", ps_sa[:, 256:512], onesf[:], flrow[0:1, 256:512],
         start=True, stop=True, inc=[(s_mm, 1)])
    k.wait("vector", s_mm)
    k.do("vector", "tensor_copy", aux2[:, :], ps_sa[:, :], inc=[(s_ve, 1)])

    if stop_after == "A4":
        pa2.close(); pa.close(); ab.close(); ctx.close(); k.build()
        return nc

    k.wait("gpsimd", s_ve, CHAIN_DONE)
    k.wait("gpsimd", s_ac)
    for qq in range(4):
        k.do("gpsimd", "dma_start",
             out=cc_src.ap()[0:2048, :].rearrange("(h n) p -> n h p", n=64)[:, qq * 8:(qq + 1) * 8, :],
             in_=rloc[0:64, (NT - 1) * NH * HP:NT * NH * HP].rearrange(
                 "n (h p) -> n h p", p=HP)[:, qq * 8:(qq + 1) * 8, :],
             inc=[(s_gp, 16)])
    k.do("gpsimd", "dma_start", out=cc_src.ap()[2048:2049, 0:32],
         in_=spscr[0:32, 0:1], inc=[(s_gp, 16)])
    k.do("vector", "memset", spscr[0:32, 1:2], 0.0, inc=[(s_ve, 1)])
    k.wait("gpsimd", s_ve)
    k.do("gpsimd", "dma_start", out=cc_src.ap()[2048:2049, 32:64],
         in_=spscr[0:32, 1:2], inc=[(s_gp, 16)])
    k.wait("gpsimd", s_gp)
    k.do("gpsimd", "collective_compute", "AllGather", AL.bypass,
         replica_groups=[[0, 1, 2, 3], [4, 5, 6, 7]],
         ins=[cc_src.ap().opt()], outs=[cc_dst.ap().opt()], inc=[(s_cc, 1)])
    k.do("vector", "memset", prows[:], 0.0, inc=[(s_ve, 1)])
    PROWS_Z = s_ve.n
    k.wait("gpsimd", s_cc, 1)
    k.wait("gpsimd", s_ve, PROWS_Z)
    for j in range(3):
        k.do("gpsimd", "dma_start", out=prows[0:1, j * 64:j * 64 + 32],
             in_=cc_dst.ap()[j * 2049 + 2048:j * 2049 + 2049, 0:32],
             inc=[(s_gp, 16)])
    GP_PROWS = s_gp.n

    if stop_after == "A":
        emit_dumps()
        pa2.close(); pa.close(); ab.close(); ctx.close(); k.build()
        return nc

    pa2.close()
    pb = ExitStack()
    ps_ya = pb.enter_context(nc.psum_tensor("ps_ya", [128, 1024], F32))
    ps_yb = pb.enter_context(nc.psum_tensor("ps_yb", [128, 1024], F32))

    k.wait("tensor", s_gp, GP_PROWS)
    k.do("tensor", "matmul", ps_sa[:, 0:192], onesf[:], prows[0:1, 0:192],
         start=True, stop=True, inc=[(s_mm, 1)])
    k.wait("vector", s_mm)
    k.do("vector", "tensor_copy", prep[:, :], ps_sa[:, 0:192], inc=[(s_ve, 1)])

    rg_rdy = {}
    ve_comb = {}
    for h in range(NH):
        slot = (h % 2) * 3 * HP
        if h >= 2:
            k.wait("gpsimd", s_ve, ve_comb[h - 2])
        for j in range(3):
            k.do("gpsimd", "dma_start",
                 out=rg[0:64, slot + j * HP:slot + (j + 1) * HP],
                 in_=cc_dst.ap()[j * 2049 + h * 64:j * 2049 + (h + 1) * 64, :],
                 inc=[(s_rg[h % 2], 16)])
        rg_rdy[h] = s_rg[h % 2].n
        r0 = rg[0:64, slot + 0 * HP:slot + 1 * HP]
        r1 = rg[0:64, slot + 1 * HP:slot + 2 * HP]
        r2 = rg[0:64, slot + 2 * HP:slot + 3 * HP]
        k.wait("vector", s_rg[h % 2], rg_rdy[h])
        k.do("vector", "scalar_tensor_tensor", i2scr[0:64, 0:HP], r0,
             prep[0:64, 64 + h:65 + h], r1, AL.mult, AL.add)
        k.do("vector", "scalar_tensor_tensor", i2scr[0:64, HP:2 * HP],
             i2scr[0:64, 0:HP], prep[0:64, 128 + h:129 + h], r2, AL.mult, AL.add)
        k.do("vector", "tensor_scalar", A(initb, h * HP, HP)[0:64, :], r0,
             aux[0:64, 65:66], None, AL.mult)
        k.do("vector", "scalar_tensor_tensor", A(initb, h * HP, HP)[0:64, :],
             i2scr[0:64, 0:HP], aux[0:64, 66:67], A(initb, h * HP, HP)[0:64, :],
             AL.mult, AL.add)
        k.do("vector", "scalar_tensor_tensor", A(initb, h * HP, HP)[0:64, :],
             i2scr[0:64, HP:2 * HP], aux[0:64, 67:68],
             A(initb, h * HP, HP)[0:64, :], AL.mult, AL.add, inc=[(s_ve, 1)])
        ve_comb[h] = s_ve.n
    VE_INIT = s_ve.n

    mt_guard = {0: 0, 1: 0}
    mt_ctr = 0
    for t in range(NT):
        for G in range(2):
            k.wait("gpsimd", s_ve, VE_INIT)
            k.wait("gpsimd", s_mm)
            k.do("gpsimd", "dma_start", out=csrow_hi[0:1, :],
                 in_=A(acsT_hi, t * CH, CH)[G * 16:G * 16 + 16, :],
                 inc=[(s_gp, 16)])
            k.do("gpsimd", "dma_start", out=csrow_lo[0:1, :],
                 in_=A(acsT_lo, t * CH, CH)[G * 16:G * 16 + 16, :],
                 inc=[(s_gp, 16)])
            GPF = s_gp.n
            k.wait("vector", s_ve, VE_INIT)
            for hh in range(16):
                h = G * 16 + hh
                if t == 0:
                    k.do("vector", "tensor_copy", A(prevt, hh * HP, HP)[0:64, :],
                         A(initb, h * HP, HP)[0:64, :])
                else:
                    k.do("vector", "scalar_tensor_tensor",
                         A(prevt, hh * HP, HP)[0:64, :],
                         A(initb, h * HP, HP)[0:64, :],
                         aux2[0:64, 256 + h * 8 + t:257 + h * 8 + t],
                         A(rloc, ((t - 1) * NH + h) * HP, HP)[0:64, :],
                         AL.mult, AL.add)
            k.do("vector", "engine_nop", inc=[(s_ve, 1)])
            PREV_G = s_ve.n

            for q4 in range(4):
                hb = q4 * 4
                pst = ps_sa if q4 % 2 == 0 else ps_sb
                k.wait("tensor", s_gp, GPF)
                k.wait("tensor", s_ve)
                k.do("tensor", "matmul", pst[:, 0:512], onesb[:],
                     csrow_hi[0:1, hb * 128:hb * 128 + 512],
                     start=True, stop=False)
                k.do("tensor", "matmul", pst[:, 0:512], onesb[:],
                     csrow_lo[0:1, hb * 128:hb * 128 + 512],
                     start=False, stop=True, inc=[(s_mm, 1)])
                MM_REP = s_mm.n
                for i4 in range(4):
                    hh = hb + i4
                    h = G * 16 + hh
                    slot = mt_ctr % 2
                    k.wait("vector", s_mm, MM_REP)
                    k.wait("vector", s_ac)
                    k.do("vector", "scalar_tensor_tensor",
                         t1f[:, slot * CH:slot * CH + CH],
                         pst[:, i4 * 128:i4 * 128 + 128],
                         A(nacs, t * NH + h, 1), m0c[:], AL.add, AL.add,
                         inc=[(s_ve, 1)])
                    k.wait("scalar", s_ve)
                    k.do("scalar", "activation",
                         t2f[:, slot * CH:slot * CH + CH],
                         t1f[:, slot * CH:slot * CH + CH], AF.Exp,
                         inc=[(s_ac, 1)])
                    k.wait("vector", s_ac)
                    k.wait("vector", s_mm, mt_guard[slot])
                    k.do("vector", "tensor_mul", A(mt, slot * CH, CH),
                         t2f[:, slot * CH:slot * CH + CH],
                         A(sct, t * CH, CH))
                    k.do("vector", "tensor_scalar", A(xdtd, slot * HP, HP),
                         A(xn, t * D + h * HP, HP), A(dtf, t * NH + h, 1), None,
                         AL.mult, inc=[(s_ve, 1)])
                    MT_RDY = s_ve.n
                    k.wait("tensor", s_ve, MT_RDY)
                    k.do("tensor", "matmul", ps_ya[:, hh * HP:hh * HP + HP],
                         A(mt, slot * CH, CH), A(xdtd, slot * HP, HP),
                         start=True, stop=True, inc=[(s_mm, 1)])
                    mt_guard[slot] = s_mm.n
                    mt_ctr += 1
            k.wait("tensor", s_ve, PREV_G)
            for hh in range(16):
                k.do("tensor", "matmul", ps_yb[:, hh * HP:hh * HP + HP],
                     A(cT, t * CH, CH)[0:64, :], A(prevt, hh * HP, HP)[0:64, :],
                     start=True, stop=True,
                     inc=[(s_mm, 1)] if hh == 15 else None)
            k.wait("vector", s_mm)
            for hh in range(16):
                h = G * 16 + hh
                k.do("vector", "tensor_scalar", yscr[:, 0:HP],
                     ps_yb[:, hh * HP:hh * HP + HP], A(sdo, t * NH + h, 1),
                     None, AL.mult)
                k.do("vector", "tensor_add", yscr[:, HP:2 * HP],
                     ps_ya[:, hh * HP:hh * HP + HP], yscr[:, 0:HP])
                k.do("vector", "scalar_tensor_tensor",
                     A(ybuf, t * D + h * HP, HP),
                     A(xn, t * D + h * HP, HP), aux[:, 32 + h:33 + h],
                     yscr[:, HP:2 * HP], AL.mult, AL.add, inc=[(s_ve, 1)])
            k.wait("tensor", s_ve)
    Y_DONE = s_ve.n

    if stop_after == "B":
        emit_dumps()
        pb.close(); pa.close(); ab.close(); ctx.close(); k.build()
        return nc

    ab.close()
    pb.close()
    pa.close()
    d2 = ExitStack()
    ytst = d2.enter_context(nc.sbuf_tensor("ytst", [128, DK * 256], BF16))
    owtb = d2.enter_context(nc.sbuf_tensor("owtb", [128, 4 * 1024], BF16))
    outbr = d2.enter_context(nc.sbuf_tensor("outbr", [1, D], BF16))
    pd = ExitStack()
    ps_u1 = pd.enter_context(nc.psum_tensor("ps_u1", [128, 128], BF16))
    ps_u2 = pd.enter_context(nc.psum_tensor("ps_u2", [128, 128], BF16))
    ps_fxd = pd.enter_context(nc.psum_tensor("ps_fxd", [128, 16], F32))
    ps_oa = pd.enter_context(nc.psum_tensor("ps_oa", [128, 1024], F32))
    ps_ob = pd.enter_context(nc.psum_tensor("ps_ob", [128, 1024], F32))

    if d_parts >= 1:
        k.do("gpsimd", "dma_start", out=outbr[:], in_=outb_hbm[:, :],
             inc=[(s_cldg, 16)])
    OUTB_LD = s_cldg.n

    nwo = 0
    tev0 = s_tev.n
    for tp in range(4 if d_parts >= 2 else 0):
        tA, tB = 2 * tp, 2 * tp + 1
        k.wait("tensor", s_ve, Y_DONE)
        k.wait("tensor", s_sev)
        nT2 = 0
        for dik in range(DK):
            for ti, tt in enumerate((tA, tB)):
                pst = ps_u1 if nT2 % 2 == 0 else ps_u2
                if nT2 >= 1:
                    k.wait("tensor", s_tev, tev0 + nT2)
                k.do("tensor", "transpose", pst[:, :],
                     A(ybuf, tt * D + dik * 128, 128), identb[:])
                k.do("tensor", "matmul", ps_fxd[0:1, 0:1], onesb[0:1, 0:1],
                     onesb[0:1, 0:1], start=True, stop=True, inc=[(s_tps, 1)])
                k.wait("vector", s_tps)
                k.do("vector", "tensor_copy",
                     A(ytst, dik * 256 + ti * 128, 128), pst[:, :],
                     inc=[(s_tev, 1)])
                nT2 += 1
        tev0 = s_tev.n
        YT_PAIR = s_tev.n
        if d_parts < 3:
            continue
        for dohalf in range(2):
            for dik in range(DK):
                if nwo >= 4:
                    k.wait("gpsimd", s_owt, nwo - 3)
                k.do("gpsimd", "dma_start",
                     out=A(owtb, (nwo % 4) * 1024, 1024),
                     in_=owt_hbm[dik * 128:(dik + 1) * 128,
                                 dohalf * 1024:(dohalf + 1) * 1024],
                     inc=[(s_ow[nwo % 4], 16)])
                k.wait("tensor", s_ow[nwo % 4], 16 * (nwo // 4 + 1))
                k.wait("tensor", s_tev, YT_PAIR)
                if dik == 0:
                    k.wait("tensor", s_cldg, OUTB_LD)
                    k.wait("tensor", s_sev)
                for ti in range(2):
                    pdst = (ps_oa, ps_ob)[ti]
                    for q in range(2):
                        k.do("tensor", "matmul", pdst[:, q * 512:(q + 1) * 512],
                             A(ytst, dik * 256 + ti * 128, 128),
                             A(owtb, (nwo % 4) * 1024 + q * 512, 512),
                             start=(dik == 0),
                             stop=(d_parts <= 3 and dik == DK - 1),
                             inc=[(s_owt, 1)] if (ti == 1 and q == 1) else None)
                nwo += 1
            for ti in range(2):
                pdst = (ps_oa, ps_ob)[ti]
                for q in range(2):
                    if d_parts >= 4:
                        k.do("tensor", "matmul", pdst[:, q * 512:(q + 1) * 512],
                             onesb[:],
                             outbr[0:1, dohalf * 1024 + q * 512:
                                   dohalf * 1024 + q * 512 + 512],
                             start=False, stop=True,
                             inc=[(s_mm, 1)] if (ti == 1 and q == 1) else None)
            if d_parts < 4:
                k.do("tensor", "nop", inc=[(s_mm, 1)])
            MM_S = s_mm.n
            k.wait("scalar", s_mm, MM_S)
            for ti, tt in enumerate((tA, tB)):
                if d_parts >= 4:
                    k.do("scalar", "copy",
                         A(ybuf, tt * D + dohalf * 1024, 512),
                         (ps_oa, ps_ob)[ti][:, 0:512])
                    k.do("scalar", "copy",
                         A(ybuf, tt * D + dohalf * 1024 + 512, 512),
                         (ps_oa, ps_ob)[ti][:, 512:1024], inc=[(s_sev, 1)])
                else:
                    k.do("scalar", "activation", spscr[0:1, 0:1] if False else A(ybuf, tt * D, 1),
                         A(ybuf, tt * D, 1), AF.Copy, inc=[(s_sev, 1)])
    S_DONE = s_sev.n

    if stop_after == "D":
        emit_dumps()
        pd.close(); d2.close(); ctx.close(); k.build()
        return nc

    d2.close()
    pd.close()
    c2 = ExitStack()
    sbc = lambda name, shape, dt: c2.enter_context(nc.sbuf_tensor(name, shape, dt))
    wgu = sbc("wgu", [128, 2 * DK * 512], BF16)
    hsp = sbc("hsp", [128, 64 * 512], BF16)
    hraw = sbc("hraw", [128, 4 * 512], BF16)
    keys = sbc("keys", [128, 6 * 256], F32)
    wdb = sbc("wdb", [128, 4 * 1024], BF16)
    gsc = sbc("gsc", [128, 2 * 1024], BF16)
    fscr = sbc("fscr", [128, 2 * 1024], F32)
    pc = ExitStack()
    ps_fa = pc.enter_context(nc.psum_tensor("ps_fa", [128, 1024], F32))
    ps_fb = pc.enter_context(nc.psum_tensor("ps_fb", [128, 1024], F32))
    ps_fc = pc.enter_context(nc.psum_tensor("ps_fc", [128, 1024], F32))
    ps_fd = pc.enter_context(nc.psum_tensor("ps_fd", [128, 1024], F32))

    WGU_FREE = {0: 0, 1: 0}
    GEV_GUARD = {}
    s_bar = Sem(nc, "dbar")
    k.do("tensor", "nop", inc=[(s_bar, 1)])
    k.do("vector", "engine_nop", inc=[(s_bar, 1)])
    k.do("scalar", "nop", inc=[(s_bar, 1)])
    k.do("gpsimd", "nop", inc=[(s_bar, 1)])
    k.do("sync", "nop", inc=[(s_bar, 1)])
    for e in ("tensor", "vector", "scalar", "gpsimd", "sync"):
        k.wait(e, s_bar, 5)
    k.wait("gpsimd", s_sev, S_DONE)
    k.wait("gpsimd", s_mm)
    WD_USED = {}
    FS_GUARD = {}
    nwd = 0
    for HF in range(2):
        tokc0 = HF * 512
        for ig in range(NIG):
            for w_i, (hbmw, off, swx) in enumerate(
                    ((wg_hbm, 0, s_wgl), (wu_hbm, DK * 512, s_wul))):
                k.wait("gpsimd", s_mm, WGU_FREE[w_i])
                for q in range(4):
                    k.do("gpsimd", "dma_start",
                         out=A(wgu, off, DK * 512).rearrange(
                             "p (k c) -> p k c", c=512)[:, q * 4:(q + 1) * 4, :],
                         in_=hbmw[:, ig * 512:(ig + 1) * 512].rearrange(
                             "(k p) c -> p k c", p=128)[:, q * 4:(q + 1) * 4, :],
                         inc=[(swx, 16)])
            WLD_G = s_wgl.n
            WLD_U = s_wul.n
            for phase in range(2):
                woff = phase * DK * 512
                pss = (ps_fa, ps_fb) if phase == 0 else (ps_fc, ps_fd)
                k.wait("tensor", s_wgl if phase == 0 else s_wul,
                       WLD_G if phase == 0 else WLD_U)
                if HF == 0 and ig == 0 and phase == 0:
                    k.wait("tensor", s_sev, S_DONE)
                if phase == 0 and (HF + ig) > 0:
                    k.wait("tensor", s_ac)
                if phase == 1 and (HF + ig) > 0:
                    k.wait("tensor", s_hev)
                for dk in range(DK):
                    for r in range(4):
                        pdst = pss[r // 2]
                        k.do("tensor", "matmul",
                             pdst[:, (r % 2) * 512:(r % 2) * 512 + 512],
                             A(wgu, woff + dk * 512, 512)[:, r::4],
                             A(xnt, dk * TOK + tokc0, 512),
                             start=(dk == 0), stop=(dk == DK - 1),
                             inc=[(s_mm, 1)] if (dk == DK - 1 and r == 3) else None)
                WGU_FREE[phase] = s_mm.n
                MMP = s_mm.n
                if phase == 0:
                    k.wait("scalar", s_mm, MMP)
                    k.wait("scalar", s_hev)
                    k.wait("scalar", s_spr)
                    for r in range(4):
                        k.do("scalar", "activation", A(hraw, r * 512, 512),
                             pss[r // 2][:, (r % 2) * 512:(r % 2) * 512 + 512],
                             AF.Silu, inc=[(s_ac, 1)] if r == 3 else None)
                else:
                    k.wait("vector", s_mm, MMP)
                    k.wait("vector", s_ac)
                    for r in range(4):
                        k.do("vector", "tensor_mul",
                             A(hsp, (ig * 4 + r) * 512, 512),
                             A(hraw, r * 512, 512),
                             pss[r // 2][:, (r % 2) * 512:(r % 2) * 512 + 512],
                             inc=[(s_hev, 1)])
            for half in range(2):
                co = half * 256
                k.wait("scalar", s_hev)
                k.wait("scalar", s_spr)
                for r in range(4):
                    k.do("scalar", "activation", A(keys, r * 256, 256),
                         A(hsp, (ig * 4 + r) * 512 + co, 256), AF.Abs,
                         scale=float(1.0 - r * NUDGE),
                         inc=[(s_ac, 1)] if r == 3 else None)
                k.wait("vector", s_ac)
                kk = [A(keys, r * 256, 256) for r in range(4)]
                thr = A(keys, 4 * 256, 256)
                k.do("vector", "tensor_max", thr, kk[0], kk[1])
                k.do("vector", "tensor_tensor", kk[0], kk[0], kk[1], AL.min)
                k.do("vector", "tensor_max", kk[1], kk[2], kk[3])
                k.do("vector", "tensor_tensor", kk[2], kk[2], kk[3], AL.min)
                k.do("vector", "tensor_tensor", kk[3], thr, kk[1], AL.min)
                k.do("vector", "tensor_max", kk[0], kk[0], kk[2])
                k.do("vector", "tensor_max", thr, kk[3], kk[0], inc=[(s_ve, 1)])
                k.wait("scalar", s_ve)
                for r in range(4):
                    k.do("scalar", "activation", A(keys, r * 256, 256),
                         A(hsp, (ig * 4 + r) * 512 + co, 256), AF.Abs,
                         scale=float(1.0 - r * NUDGE),
                         inc=[(s_ac, 1)] if r == 3 else None)
                k.wait("vector", s_ac)
                for r in range(4):
                    k.do("vector", "tensor_tensor", keys[:, 5 * 256:6 * 256],
                         kk[r], thr, AL.is_ge)
                    k.do("vector", "tensor_mul", A(hraw, r * 512, 256),
                         A(hsp, (ig * 4 + r) * 512 + co, 256),
                         keys[:, 5 * 256:6 * 256])
                    k.do("vector", "tensor_copy",
                         A(hsp, (ig * 4 + r) * 512 + co, 256),
                         A(hraw, r * 512, 256),
                         inc=[(s_spr, 1)] if r == 3 else None)
        SP_HALF = s_spr.n

        pslc = [(ps_fa, 0), (ps_fa, 512), (ps_fb, 0), (ps_fb, 512),
                (ps_fc, 0), (ps_fc, 512), (ps_fd, 0), (ps_fd, 512)]
        for dohalf in range(2):
            k.wait("tensor", s_st[0], s_st[0].n)
            k.wait("tensor", s_st[1], s_st[1].n)
            k.wait("tensor", s_gev)
            k.wait("tensor", s_ac)
            k.wait("tensor", s_ve)
            for tc in range(4):
                tglob = HF * 4 + tc
                key = (tc, (HF * 2 + dohalf) - 1)
                if key in GEV_GUARD:
                    k.wait("sync", s_gev, GEV_GUARD[key])
                k.do("sync", "dma_start",
                     out=xstage[:, tc * 1024:(tc + 1) * 1024],
                     in_=x_hbm[tglob * 128:(tglob + 1) * 128,
                               dohalf * 1024:(dohalf + 1) * 1024],
                     inc=[(s_xld2, 16)])
            XLD_H = s_xld2.n
            k.wait("tensor", s_spr, SP_HALF)
            k.wait("tensor", s_gev)
            k.wait("tensor", s_ac)
            k.wait("tensor", s_hev)
            for u in range(64):
                ig, r = u // 4, u % 4
                if nwd >= 1:
                    k.wait("gpsimd", s_mm, WD_USED[nwd - 1])
                k.do("gpsimd", "dma_start",
                     out=A(wdb, (nwd % 4) * 1024, 1024),
                     in_=wd_hbm[(ig * 4 + r) * 128:(ig * 4 + r) * 128 + 128,
                                dohalf * 1024:(dohalf + 1) * 1024],
                     inc=[(s_wd[nwd % 4], 16)])
                k.wait("tensor", s_wd[nwd % 4], 16 * (nwd // 4 + 1))
                for tc in range(4):
                    for q in range(2):
                        pdst, pc0 = pslc[tc * 2 + q]
                        k.do("tensor", "matmul", pdst[:, pc0:pc0 + 512],
                             A(hsp, (ig * 4 + r) * 512 + tc * 128, 128),
                             A(wdb, (nwd % 4) * 1024 + q * 512, 512),
                             start=(u == 0), stop=(u == 63),
                             inc=[(s_mm, 1)] if (tc == 3 and q == 1) else None)
                WD_USED[nwd] = s_mm.n
                nwd += 1
            MM_DOWN = s_mm.n
            k.wait("scalar", s_mm, MM_DOWN)
            k.wait("scalar", s_wgl)
            k.wait("scalar", s_wul)
            for tc in range(4):
                tglob = HF * 4 + tc
                pdst = pslc[tc * 2][0]
                k.wait("scalar", s_gev)
                k.do("scalar", "activation", A(gsc, (tc % 2) * 1024, 1024),
                     pdst[:, 0:1024], AF.Silu, inc=[(s_ac, 1)])
                k.wait("vector", s_ac)
                k.wait("vector", s_xld2, XLD_H)
                k.wait("vector", s_st[tc % 2], FS_GUARD.get((tc % 2), 0))
                k.do("vector", "tensor_mul", A(fscr, (tc % 2) * 1024, 1024),
                     A(gsc, (tc % 2) * 1024, 1024),
                     A(ybuf, tglob * D + dohalf * 1024, 1024))
                k.do("vector", "tensor_add", A(fscr, (tc % 2) * 1024, 1024),
                     A(fscr, (tc % 2) * 1024, 1024),
                     xstage[:, tc * 1024:(tc + 1) * 1024],
                     inc=[(s_gev, 1)])
                GEV_GUARD[(tc, HF * 2 + dohalf)] = s_gev.n
                k.wait("scalar", s_gev)
                k.do("scalar", "dma_start",
                     out=out_hbm[tglob * 128:(tglob + 1) * 128,
                                 dohalf * 1024:(dohalf + 1) * 1024],
                     in_=A(fscr, (tc % 2) * 1024, 1024), inc=[(s_st[tc % 2], 16)])
                FS_GUARD[(tc % 2)] = s_st[tc % 2].n

    dbg_map.update(hsp=hsp, gsc=gsc, fscr=fscr, keys=keys, hraw=hraw)
    emit_dumps()
    k.wait("scalar", s_st[0], s_st[0].n)
    k.wait("scalar", s_st[1], s_st[1].n)
    k.wait("sync", s_st[0], s_st[0].n)
    k.wait("sync", s_st[1], s_st[1].n)

    pc.close(); c2.close(); ctx.close()
    k.build()
    return nc


def _layout_inputs(inputs, core):
    x = np.ascontiguousarray(np.asarray(inputs["L0_attn_x"]).reshape(8192, D))
    wg = np.ascontiguousarray(np.asarray(inputs["L0_ffn_w_gate"]))
    wu = np.ascontiguousarray(np.asarray(inputs["L0_ffn_w_up"]))
    wd = np.asarray(inputs["L0_ffn_w_down"])
    idx = np.empty(II, np.int64)
    idx.reshape(NIG, 4, 128)[:] = (np.arange(NIG)[:, None, None] * 512
                                   + np.arange(4)[None, :, None]
                                   + 4 * np.arange(128)[None, None, :])
    wd_perm = np.ascontiguousarray(wd[idx])
    ow_t = np.ascontiguousarray(np.asarray(inputs["L0_attn_out_w"]).T)
    bcdt = np.ascontiguousarray(np.concatenate(
        [np.asarray(inputs["L0_attn_B_w"]).T, np.asarray(inputs["L0_attn_C_w"]).T,
         np.asarray(inputs["L0_attn_dt_w"]).T], axis=1))
    bcdt_b = np.concatenate(
        [np.asarray(inputs["L0_attn_B_b"]), np.asarray(inputs["L0_attn_C_b"]),
         np.asarray(inputs["L0_attn_dt_b"])])[None, :].astype(np.float32)
    out_b = np.asarray(inputs["L0_attn_out_b"])[None, :].astype(np.float32)
    a_log = np.asarray(inputs["L0_attn_A_log"])[None, :].astype(np.float32)
    dvec = np.asarray(inputs["L0_attn_D"])[None, :].astype(np.float32)
    ident = np.eye(128, dtype=np.float32)
    iu = np.arange(128)
    m0 = np.where(iu[None, :] >= iu[:, None], 0.0, NEG).astype(np.float32)
    ones_row = np.ones((1, 128), np.float32)
    csel = np.zeros((1, 4), np.float32)
    csel[0, core % 4] = 1.0
    return {
        "x_own": np.ascontiguousarray(x[core * TOK:(core + 1) * TOK]),
        "wg": wg, "wu": wu, "wd_perm": wd_perm, "ow_t": ow_t,
        "bcdt_wt": bcdt, "bcdt_b": bcdt_b, "out_b": out_b,
        "a_log": a_log, "dvec": dvec, "ident128": ident, "m0": m0,
        "ones_row": ones_row, "csel": csel,
    }


def kernel(**inputs):
    nc = build_program()
    in_maps = [_layout_inputs(inputs, c) for c in range(8)]
    res = run_bass_kernel_spmd(nc, in_maps, core_ids=list(range(8)))
    out = np.concatenate([r["out"] for r in res.results], axis=0)
    return out.reshape(2, 4096, D)


# version 09 (final)
# speedup vs baseline: 1.3745x; 1.3745x over previous
import numpy as np
import ml_dtypes
from contextlib import ExitStack
import concourse.bass as bass
import concourse.mybir as mybir
from concourse.bass_utils import run_bass_kernel_spmd

F32 = mybir.dt.float32
BF16 = mybir.dt.bfloat16
AL = mybir.AluOpType
AF = mybir.ActivationFunctionType
BF = ml_dtypes.bfloat16

D = 2048
NH = 32
HP = 64
CH = 128
TOK = 1024
NT = 8
DK = 16
II = 8192
NIG = 16
EPS = 1e-5
NEG = -1e30
NUDGE = 2.0 ** -20


class Sem:
    def __init__(self, nc, name):
        self.h = nc.alloc_semaphore(name)
        self.n = 0


class K:
    ENGS = ("sync", "tensor", "vector", "scalar", "gpsimd")

    def __init__(self, nc):
        self.nc = nc
        self.ops = {e: [] for e in self.ENGS}

    def do(self, eng, fname, *args, inc=None, **kwargs):
        self.ops[eng].append(("op", fname, args, kwargs, inc))
        if inc is not None:
            for s, kk in inc:
                s.n += kk

    def wait(self, eng, sem, val=None):
        v = sem.n if val is None else val
        assert v <= sem.n, "wait target not yet produced at script time"
        self.ops[eng].append(("wait", sem, v))

    def _replay(self, eng_name, eng):
        for rec in self.ops[eng_name]:
            if rec[0] == "wait":
                _, sem, val = rec
                if val > 0:
                    eng.wait_ge(sem.h, val)
            else:
                _, fname, args, kwargs, inc = rec
                instr = getattr(eng, fname)(*args, **kwargs)
                if inc is not None:
                    for s, kk in inc:
                        instr = instr.then_inc(s.h, kk)

    def build(self):
        with self.nc.Block() as block:
            block.sync(lambda e: self._replay("sync", e))
            block.tensor(lambda e: self._replay("tensor", e))
            block.vector(lambda e: self._replay("vector", e))
            block.scalar(lambda e: self._replay("scalar", e))
            block.gpsimd(lambda e: self._replay("gpsimd", e))


def build_program(stop_after=None, detect_races=True, dbg=(), chunk_steps=99, d_parts=99):
    nc = bass.Bass("TRN2", target_bir_lowering=False,
                   dynamic_dma_scratch_size=16384,
                   detect_race_conditions=detect_races)

    def din(name, shape, dt=F32):
        return nc.dram_tensor(name, list(shape), dt, kind="ExternalInput").ap()

    x_hbm = din("x_own", [TOK, D])
    wg_hbm = din("wg", [D, II])
    wu_hbm = din("wu", [D, II])
    wd_hbm = din("wd_perm", [II, D])
    owt_hbm = din("ow_t", [D, D])
    bcdt_hbm = din("bcdt_wt", [D, 160])
    bcdtb_hbm = din("bcdt_b", [1, 160])
    outb_hbm = din("out_b", [1, D])
    alog_hbm = din("a_log", [1, NH])
    dvec_hbm = din("dvec", [1, NH])
    ident_hbm = din("ident128", [128, 128])
    m0_hbm = din("m0", [128, 128])
    ones_hbm = din("ones_row", [1, 128])
    csel_hbm = din("csel", [1, 4])
    out_hbm = nc.dram_tensor("out", [TOK, D], F32, kind="ExternalOutput").ap()

    cc_src = nc.dram_tensor("cc_src", [2049, 64], F32, kind="Internal")
    cc_dst = nc.dram_tensor("cc_dst", [4 * 2049, 64], F32, kind="Internal")

    ctx = ExitStack()
    sb = lambda name, shape, dt: ctx.enter_context(nc.sbuf_tensor(name, shape, dt))

    xnt = sb("xnt", [128, DK * TOK], BF16)
    ybuf = sb("ybuf", [128, NT * D], BF16)
    xstage = sb("xstage", [128, 2 * D], F32)
    identb = sb("identb", [128, 128], BF16)
    onesf = sb("onesf", [1, 128], F32)
    onesb = sb("onesb", [1, 128], BF16)

    k = K(nc)
    s_cld = Sem(nc, "cld")
    s_cldg = Sem(nc, "cldg")
    s_xld = [Sem(nc, "xld0"), Sem(nc, "xld1")]
    s_lnv = Sem(nc, "lnv")
    s_lna = Sem(nc, "lna")
    s_tps = Sem(nc, "tps")
    s_tev = Sem(nc, "tev")
    s_mm = Sem(nc, "smm")
    s_ve = Sem(nc, "sve")
    s_ac = Sem(nc, "sac")
    s_gp = Sem(nc, "sgp")
    s_rg = [Sem(nc, "srg0"), Sem(nc, "srg1")]
    s_cc = Sem(nc, "scc")
    s_wgl = Sem(nc, "wgl")
    s_wul = Sem(nc, "wul")
    s_wd = [Sem(nc, "swd%d" % i) for i in range(4)]
    s_ow = [Sem(nc, "sow%d" % i) for i in range(4)]
    s_st = [Sem(nc, "sst0"), Sem(nc, "sst1")]
    s_hev = Sem(nc, "hev")
    s_spr = Sem(nc, "spr")
    s_gev = Sem(nc, "gev")
    s_xld2 = Sem(nc, "xld2")
    s_owt = Sem(nc, "owt")
    s_sev = Sem(nc, "sev")

    def A(t, c0, w):
        return t[:, c0:c0 + w]

    ab = ExitStack()
    sba = lambda name, shape, dt: ab.enter_context(nc.sbuf_tensor(name, shape, dt))
    xn = sba("xn", [128, NT * D], BF16)
    identf = sba("identf", [128, 128], F32)
    m0c = sba("m0c", [128, 128], F32)
    aux = sba("aux", [128, 128], F32)
    aux2 = sba("aux2", [128, 512], F32)
    lncol = sba("lncol", [128, 48], F32)
    stat6 = sba("stat6", [128, 48], F32)
    rows = sba("rows", [1, 256], F32)
    bcdtw = sba("bcdtw", [128, DK * 160], BF16)
    bcdtb = sba("bcdtb", [1, 160], BF16)
    btok = sba("btok", [128, NT * HP], BF16)
    ctok = sba("ctok", [128, NT * HP], BF16)
    bT = sba("bT", [64, NT * CH], BF16)
    cT = sba("cT", [64, NT * CH], BF16)
    dtf = sba("dtf", [128, NT * NH], F32)
    dtds = sba("dtds", [128, NT * NH], F32)
    acs = sba("acs", [128, NT * NH], F32)
    nacs = sba("nacs", [128, NT * NH], F32)
    sdo = sba("sdo", [128, NT * NH], F32)
    acsT = sba("acsT", [32, CH], F32)
    acsT_hi = sba("acsT_hi", [32, NT * CH], BF16)
    acsT_lo = sba("acsT_lo", [32, NT * CH], BF16)
    onest = sba("onest", [32, CH], F32)
    alast = sba("alast", [32, NT], F32)
    pincl = sba("pincl", [32, NT], F32)
    pexcl = sba("pexcl", [32, NT], F32)
    flrow = sba("flrow", [1, 544], F32)
    csrow_hi = sba("csrow_hi", [1, 2048], BF16)
    csrow_lo = sba("csrow_lo", [1, 2048], BF16)
    sct = sba("sct", [128, NT * CH], BF16)
    spscr = sba("spscr", [128, 96], F32)
    datile = sba("datile", [128, NH], F32)
    rloc = sba("rloc", [64, NT * NH * HP], BF16)
    xdtp = sba("xdtp", [128, 2 * CH], BF16)
    xdtd = sba("xdtd", [128, 4 * HP], BF16)
    mt = sba("mt", [128, 4 * CH], BF16)
    t1f = sba("t1f", [128, 2 * CH], F32)
    t2f = sba("t2f", [128, 2 * CH], F32)
    yscr = sba("yscr", [128, 2 * HP], F32)
    rg = sba("rg", [64, 8 * HP], F32)
    prows = sba("prows", [1, 192], F32)
    prep = sba("prep", [128, 192], F32)
    initb = sba("initb", [64, NH * HP], F32)
    i2scr = sba("i2scr", [64, 2 * HP], F32)
    prevt = sba("prevt", [64, 16 * HP], BF16)

    dbg_map = dict(flrow=flrow, pincl=pincl, pexcl=pexcl, alast=alast,
               xn=xn, xnt=xnt, btok=btok, ctok=ctok, bT=bT, cT=cT,
                   dtf=dtf, dtds=dtds, acs=acs, nacs=nacs, sdo=sdo,
                   acsT_hi=acsT_hi, sct=sct, rloc=rloc,
                   aux=aux, aux2=aux2, prep=prep, initb=initb, ybuf=ybuf,
                   prevt=prevt, xstage=xstage)

    def emit_dumps():
        for name in dbg:
            if name in ("cc_src", "cc_dst"):
                t = {"cc_src": cc_src, "cc_dst": cc_dst}[name]
                dout = nc.dram_tensor("dbg_" + name, list(t.shape), F32,
                                      kind="ExternalOutput").ap()
                for s in (s_ve, s_ac, s_mm, s_gp):
                    k.wait("gpsimd", s)
                k.do("gpsimd", "dma_start", out=dout[:, :], in_=t.ap()[:, :],
                     inc=[(s_gp, 16)])
                k.wait("gpsimd", s_gp)
                continue
            t = dbg_map[name]
            shp = list(t.shape)
            dout = nc.dram_tensor("dbg_" + name, shp, F32,
                                  kind="ExternalOutput").ap()
            for s in (s_ve, s_ac, s_mm, s_tev, s_lna):
                k.wait("gpsimd", s)
            nrow = shp[0]
            nspl = max(1, (nrow * max(1, shp[1] // 8192)) // 512)
            k.do("gpsimd", "dma_start", out=dout[:, :], in_=t[:, :],
                 inc=[(s_gp, 16)])
            k.wait("gpsimd", s_gp)

    pa = ExitStack()
    ps_sa = pa.enter_context(nc.psum_tensor("ps_sa", [128, 512], F32))
    ps_sb = pa.enter_context(nc.psum_tensor("ps_sb", [128, 512], F32))
    ps_sd = pa.enter_context(nc.psum_tensor("ps_sd", [128, 512], F32))
    pa2 = ExitStack()
    ps_t1 = pa2.enter_context(nc.psum_tensor("ps_t1", [128, 128], BF16))
    ps_t2 = pa2.enter_context(nc.psum_tensor("ps_t2", [128, 128], BF16))
    ps_fx = pa2.enter_context(nc.psum_tensor("ps_fx", [128, 16], F32))

    k.do("gpsimd", "dma_start", out=identb[:], in_=ident_hbm[:, :], inc=[(s_cldg, 16)])
    k.do("sync", "dma_start", out=m0c[:], in_=m0_hbm[:, :], inc=[(s_cld, 16)])
    k.do("sync", "dma_start", out=identf[:], in_=ident_hbm[:, :], inc=[(s_cld, 16)])
    k.do("sync", "dma_start", out=onesf[:], in_=ones_hbm[:, :], inc=[(s_cld, 16)])
    k.do("sync", "dma_start", out=rows[0:1, 0:32], in_=alog_hbm[:, :], inc=[(s_cld, 16)])
    k.do("sync", "dma_start", out=rows[0:1, 32:64], in_=dvec_hbm[:, :], inc=[(s_cld, 16)])
    k.do("sync", "dma_start", out=rows[0:1, 64:68], in_=csel_hbm[:, :], inc=[(s_cld, 16)])
    k.do("gpsimd", "dma_start", out=onesb[:], in_=ones_hbm[:, :], inc=[(s_cldg, 16)])
    for qq in range(4):
        k.do("gpsimd", "dma_start",
             out=bcdtw[:].rearrange("p (k c) -> p k c", c=160)[:, qq * 4:(qq + 1) * 4, :],
             in_=bcdt_hbm[:, :].rearrange("(k p) c -> p k c", p=128)[:, qq * 4:(qq + 1) * 4, :],
             inc=[(s_cldg, 16)])
    k.do("gpsimd", "dma_start", out=bcdtb[:], in_=bcdtb_hbm[:, :], inc=[(s_cldg, 16)])
    CLD = s_cld.n
    CLDG = s_cldg.n

    k.wait("scalar", s_cld, CLD)
    k.wait("scalar", s_cldg, CLDG)
    k.do("scalar", "activation", rows[0:1, 128:160], rows[0:1, 0:32], AF.Exp,
         inc=[(s_ac, 1)])
    k.wait("tensor", s_ac)
    k.wait("tensor", s_cld, CLD)
    k.wait("tensor", s_cldg, CLDG)
    k.do("tensor", "matmul", ps_sa[:, 0:32], onesf[:], rows[0:1, 128:160],
         start=True, stop=True)
    k.do("tensor", "matmul", ps_sa[:, 32:64], onesf[:], rows[0:1, 32:64],
         start=True, stop=True)
    k.do("tensor", "matmul", ps_sa[:, 64:68], onesf[:], rows[0:1, 64:68],
         start=True, stop=True, inc=[(s_mm, 1)])
    k.wait("vector", s_mm)
    k.do("vector", "tensor_scalar", aux[:, 0:32], ps_sa[:, 0:32], -1.0, None, AL.mult)
    k.do("vector", "tensor_copy", aux[:, 32:68], ps_sa[:, 32:68])
    k.do("vector", "memset", onest[:], 1.0, inc=[(s_ve, 1)])
    VE_AUX = s_ve.n

    lna_tile = []
    for t in range(NT):
        sl = (t % 2) * D
        if t >= 2:
            k.wait("sync", s_lna, lna_tile[t - 2])
        k.do("sync", "dma_start", out=A(xstage, sl, D),
             in_=x_hbm[t * 128:(t + 1) * 128, :], inc=[(s_xld[t % 2], 16)])
        c = 6 * t
        st6 = stat6[:, (t % 2) * 24:(t % 2) * 24 + 24]
        k.wait("vector", s_xld[t % 2], 16 * (t // 2 + 1))
        for q in range(4):
            k.do("vector", "bn_stats", st6[:, q * 6:(q + 1) * 6],
                 A(xstage, sl + q * 512, 512),
                 inc=[(s_lnv, 1)] if q == 3 else None)
        k.wait("vector", s_lnv)
        k.do("vector", "bn_aggr", lncol[:, c:c + 2], st6, inc=[(s_lnv, 1)])
        k.wait("vector", s_lnv)
        k.do("vector", "tensor_scalar", lncol[:, c + 2:c + 3],
             lncol[:, c + 1:c + 2], float(EPS), None, AL.add, inc=[(s_lnv, 1)])
        k.wait("scalar", s_lnv)
        k.do("scalar", "activation", lncol[:, c + 3:c + 4], lncol[:, c + 2:c + 3],
             AF.Ln)
        k.do("scalar", "activation", lncol[:, c + 4:c + 5], lncol[:, c + 3:c + 4],
             AF.Exp, scale=-0.5, inc=[(s_lna, 1)])
        k.wait("vector", s_lna)
        k.do("vector", "scalar_tensor_tensor", lncol[:, c + 5:c + 6],
             lncol[:, c:c + 1], -1.0, lncol[:, c + 4:c + 5], AL.mult, AL.mult,
             inc=[(s_lnv, 1)])
        k.do("vector", "tensor_scalar", A(xn, t * D, D), A(xstage, sl, D),
             lncol[:, c + 4:c + 5], lncol[:, c + 5:c + 6], AL.mult, AL.add,
             inc=[(s_lna, 1)])
        lna_tile.append(s_lna.n)

    if stop_after == "A1":
        pa2.close(); pa.close(); ab.close(); ctx.close(); k.build()
        return nc

    k.wait("tensor", s_cld, CLD)
    k.wait("tensor", s_cldg, CLDG)
    nT = 0
    for t in range(NT):
        k.wait("tensor", s_lna, lna_tile[t])
        for dk in range(DK):
            pst = ps_t1 if nT % 2 == 0 else ps_t2
            if nT >= 2:
                k.wait("tensor", s_tev, nT - 1)
            k.do("tensor", "transpose", pst[:, :],
                 A(xn, t * D + dk * 128, 128), identb[:])
            k.do("tensor", "matmul", ps_fx[0:1, 0:1], onesb[0:1, 0:1],
                 onesb[0:1, 0:1], start=True, stop=True, inc=[(s_tps, 1)])
            k.wait("vector", s_tps)
            k.do("vector", "tensor_copy", A(xnt, dk * TOK + t * 128, 128),
                 pst[:, :], inc=[(s_tev, 1)])
            nT += 1

    if stop_after == "A2":
        pa2.close(); pa.close(); ab.close(); ctx.close(); k.build()
        return nc

    for t in range(NT):
        k.wait("tensor", s_tev, (t + 1) * DK)
        if t > 0:
            k.wait("tensor", s_ac)
            k.wait("tensor", s_ve)
        for dk in range(DK):
            k.do("tensor", "matmul", ps_sb[:, 0:160],
                 A(xnt, dk * TOK + t * 128, 128), A(bcdtw, dk * 160, 160),
                 start=(dk == 0), stop=False)
        k.do("tensor", "matmul", ps_sb[:, 0:160], onesb[:], bcdtb[:],
             start=False, stop=True, inc=[(s_mm, 1)])
        MM_PROJ = s_mm.n
        k.wait("vector", s_mm, MM_PROJ)
        k.do("vector", "tensor_copy", A(btok, t * HP, HP), ps_sb[:, 0:64])
        k.do("vector", "tensor_copy", A(ctok, t * HP, HP), ps_sb[:, 64:128],
             inc=[(s_ve, 1)])
        if chunk_steps < 2:
            continue
        k.wait("scalar", s_mm, MM_PROJ)
        k.wait("scalar", s_ve)
        k.do("scalar", "activation", spscr[:, 0:32], ps_sb[:, 128:160], AF.Abs)
        k.do("scalar", "activation", spscr[:, 32:64], spscr[:, 0:32], AF.Exp,
             scale=-1.0)
        k.do("scalar", "activation", spscr[:, 64:96], spscr[:, 32:64], AF.Ln,
             bias=1.0)
        k.do("scalar", "activation", spscr[:, 0:32], ps_sb[:, 128:160], AF.Relu,
             inc=[(s_ac, 1)])
        k.wait("vector", s_ac)
        k.do("vector", "tensor_add", A(dtf, t * NH, NH), spscr[:, 0:32],
             spscr[:, 64:96])
        k.wait("vector", s_ve, VE_AUX)
        k.do("vector", "tensor_mul", datile[:], A(dtf, t * NH, NH), aux[:, 0:32],
             inc=[(s_ve, 1)])
        if chunk_steps < 3:
            continue
        k.wait("tensor", s_ve)
        k.do("tensor", "transpose", ps_sa[0:32, 128:256],
             datile[:], identf[:])
        k.do("tensor", "matmul", ps_fx[0:1, 0:1], onesb[0:1, 0:1],
             onesb[0:1, 0:1], start=True, stop=True, inc=[(s_tps, 1)])
        k.wait("vector", s_tps)
        k.do("vector", "tensor_tensor_scan", acsT[:, :], onest[:],
             ps_sa[0:32, 128:256], 0.0, AL.mult, AL.add, inc=[(s_ve, 1)])
        k.wait("vector", s_ve)
        k.wait("tensor", s_ve)
        k.do("tensor", "transpose", ps_sa[:, 256:288], acsT[:, :],
             identf[0:32, 0:32])
        k.do("tensor", "matmul", ps_fx[0:1, 0:1], onesb[0:1, 0:1],
             onesb[0:1, 0:1], start=True, stop=True, inc=[(s_tps, 1)])
        k.wait("vector", s_tps)
        k.do("vector", "tensor_copy", A(acs, t * NH, NH), ps_sa[:, 256:288])
        k.do("vector", "tensor_scalar", A(nacs, t * NH, NH), ps_sa[:, 256:288],
             -1.0, None, AL.mult)
        k.do("vector", "tensor_copy", alast[:, t:t + 1], acsT[:, 127:128],
             inc=[(s_ve, 1)])
        if chunk_steps < 4:
            continue
        VE_ACS = s_ve.n
        k.wait("scalar", s_ve, VE_ACS)
        k.do("scalar", "activation", A(acsT_hi, t * CH, CH)[0:32, :], acsT[:, :],
             AF.Copy)
        k.do("scalar", "activation", A(sdo, t * NH, NH), A(acs, t * NH, NH),
             AF.Exp, inc=[(s_ac, 1)])
        k.wait("vector", s_ac)
        k.do("vector", "tensor_sub", A(acsT_lo, t * CH, CH)[0:32, :], acsT[:, :],
             A(acsT_hi, t * CH, CH)[0:32, :], inc=[(s_ve, 1)])
        if chunk_steps < 5:
            continue
        k.wait("gpsimd", s_ve, VE_ACS)
        k.wait("gpsimd", s_mm)
        k.do("gpsimd", "dma_start", out=flrow[0:1, 512:544],
             in_=alast[:, t:t + 1], inc=[(s_gp, 16)])
        k.wait("tensor", s_gp)
        k.wait("tensor", s_ve)
        k.do("tensor", "matmul", ps_sa[:, 288:320], onesf[:], flrow[0:1, 512:544],
             start=True, stop=True, inc=[(s_mm, 1)])
        k.wait("vector", s_mm)
        k.do("vector", "tensor_sub", t1f[:, 0:32], ps_sa[:, 288:320],
             A(acs, t * NH, NH), inc=[(s_ve, 1)])
        k.wait("scalar", s_ve)
        k.do("scalar", "activation", t2f[:, 0:32], t1f[:, 0:32], AF.Exp)
        k.do("scalar", "activation", t2f[:, 32:64], ps_sa[:, 288:320], AF.Exp,
             inc=[(s_ac, 1)])
        k.wait("vector", s_ac)
        k.do("vector", "tensor_mul", A(dtds, t * NH, NH), A(dtf, t * NH, NH),
             t2f[:, 0:32], inc=[(s_ve, 1)])
        VE_DTDS = s_ve.n

        if chunk_steps < 6:
            continue
        k.wait("tensor", s_ve, VE_DTDS)
        k.wait("tensor", s_tev)
        k.do("tensor", "transpose", ps_t1[0:64, :], A(btok, t * HP, HP),
             identb[:])
        k.do("tensor", "transpose", ps_t2[0:64, :], A(ctok, t * HP, HP),
             identb[:])
        k.do("tensor", "matmul", ps_fx[0:1, 0:1], onesb[0:1, 0:1],
             onesb[0:1, 0:1], start=True, stop=True, inc=[(s_tps, 1)])
        k.wait("vector", s_tps)
        k.do("vector", "tensor_copy", A(bT, t * CH, CH)[0:64, :], ps_t1[0:64, :])
        k.do("vector", "tensor_copy", A(cT, t * CH, CH)[0:64, :], ps_t2[0:64, :],
             inc=[(s_tev, 1)])
        k.wait("tensor", s_tev)
        k.do("tensor", "matmul", ps_sd[:, 0:128], A(bT, t * CH, CH)[0:64, :],
             A(cT, t * CH, CH)[0:64, :], start=True, stop=True, inc=[(s_mm, 1)])
        k.wait("vector", s_mm)
        k.do("vector", "tensor_copy", A(sct, t * CH, CH), ps_sd[:, 0:128],
             inc=[(s_ve, 1)])
        SCT_EV = s_ve.n

        if chunk_steps < 7:
            continue
        for hp2 in range(16):
            h0, h1 = 2 * hp2, 2 * hp2 + 1
            k.wait("vector", s_ve, VE_DTDS)
            k.do("vector", "tensor_scalar", A(xdtp, 0, HP),
                 A(xn, t * D + h0 * HP, HP), A(dtds, t * NH + h0, 1), None,
                 AL.mult)
            k.do("vector", "tensor_scalar", A(xdtp, HP, HP),
                 A(xn, t * D + h1 * HP, HP), A(dtds, t * NH + h1, 1), None,
                 AL.mult, inc=[(s_ve, 1)])
            k.wait("tensor", s_ve)
            pst, pc0 = (ps_sa, 320) if hp2 % 2 == 0 else (ps_sd, 256)
            k.do("tensor", "matmul", pst[0:64, pc0:pc0 + 128],
                 A(btok, t * HP, HP), xdtp[:, 0:128],
                 start=True, stop=True, inc=[(s_mm, 1)])
            k.wait("vector", s_mm)
            k.wait("vector", s_ac)
            for hh, hx in ((h0, 0), (h1, 1)):
                pcol = pc0 + hx * HP
                if t == 0:
                    k.do("vector", "tensor_copy",
                         A(rloc, (t * NH + hh) * HP, HP)[0:64, :],
                         pst[0:64, pcol:pcol + HP], inc=[(s_ve, 1)])
                else:
                    k.do("vector", "scalar_tensor_tensor",
                         A(rloc, (t * NH + hh) * HP, HP)[0:64, :],
                         A(rloc, ((t - 1) * NH + hh) * HP, HP)[0:64, :],
                         t2f[0:64, 32 + hh:33 + hh],
                         pst[0:64, pcol:pcol + HP], AL.mult, AL.add,
                         inc=[(s_ve, 1)])
    CHAIN_DONE = s_ve.n

    if stop_after == "A3":
        pa2.close(); pa.close(); ab.close(); ctx.close(); k.build()
        return nc

    k.do("vector", "tensor_tensor_scan", pincl[:, :], onest[:, 0:NT], alast[:, :],
         0.0, AL.mult, AL.add, inc=[(s_ve, 1)])
    k.wait("vector", s_ve)
    k.do("vector", "tensor_sub", pexcl[:, :], pincl[:, :], alast[:, :],
         inc=[(s_ve, 1)])
    k.wait("gpsimd", s_ve)
    k.do("gpsimd", "dma_start", out=flrow[0:1, 0:256], in_=alast[:, :],
         inc=[(s_gp, 16)])
    k.do("gpsimd", "dma_start", out=flrow[0:1, 256:512], in_=pexcl[:, :],
         inc=[(s_gp, 16)])
    k.wait("scalar", s_gp)
    k.wait("scalar", s_ve)
    k.do("scalar", "activation", flrow[0:1, 0:256], flrow[0:1, 0:256], AF.Exp)
    k.do("scalar", "activation", flrow[0:1, 256:512], flrow[0:1, 256:512], AF.Exp)
    k.do("scalar", "activation", spscr[0:32, 0:1], pincl[:, 7:8], AF.Exp,
         inc=[(s_ac, 1)])
    k.wait("tensor", s_ac)
    k.wait("tensor", s_ve)
    k.do("tensor", "matmul", ps_sa[:, 0:256], onesf[:], flrow[0:1, 0:256],
         start=True, stop=True)
    k.do("tensor", "matmul", ps_sa[:, 256:512], onesf[:], flrow[0:1, 256:512],
         start=True, stop=True, inc=[(s_mm, 1)])
    k.wait("vector", s_mm)
    k.do("vector", "tensor_copy", aux2[:, :], ps_sa[:, :], inc=[(s_ve, 1)])

    if stop_after == "A4":
        pa2.close(); pa.close(); ab.close(); ctx.close(); k.build()
        return nc

    k.wait("gpsimd", s_ve, CHAIN_DONE)
    k.wait("gpsimd", s_ac)
    for qq in range(4):
        k.do("gpsimd", "dma_start",
             out=cc_src.ap()[0:2048, :].rearrange("(h n) p -> n h p", n=64)[:, qq * 8:(qq + 1) * 8, :],
             in_=rloc[0:64, (NT - 1) * NH * HP:NT * NH * HP].rearrange(
                 "n (h p) -> n h p", p=HP)[:, qq * 8:(qq + 1) * 8, :],
             inc=[(s_gp, 16)])
    k.do("gpsimd", "dma_start", out=cc_src.ap()[2048:2049, 0:32],
         in_=spscr[0:32, 0:1], inc=[(s_gp, 16)])
    k.do("vector", "memset", spscr[0:32, 1:2], 0.0, inc=[(s_ve, 1)])
    k.wait("gpsimd", s_ve)
    k.do("gpsimd", "dma_start", out=cc_src.ap()[2048:2049, 32:64],
         in_=spscr[0:32, 1:2], inc=[(s_gp, 16)])
    k.wait("gpsimd", s_gp)
    k.do("gpsimd", "collective_compute", "AllGather", AL.bypass,
         replica_groups=[[0, 1, 2, 3], [4, 5, 6, 7]],
         ins=[cc_src.ap().opt()], outs=[cc_dst.ap().opt()], inc=[(s_cc, 1)])
    k.do("vector", "memset", prows[:], 0.0, inc=[(s_ve, 1)])
    PROWS_Z = s_ve.n
    k.wait("gpsimd", s_cc, 1)
    k.wait("gpsimd", s_ve, PROWS_Z)
    for j in range(3):
        k.do("gpsimd", "dma_start", out=prows[0:1, j * 64:j * 64 + 32],
             in_=cc_dst.ap()[j * 2049 + 2048:j * 2049 + 2049, 0:32],
             inc=[(s_gp, 16)])
    GP_PROWS = s_gp.n

    if stop_after == "A":
        emit_dumps()
        pa2.close(); pa.close(); ab.close(); ctx.close(); k.build()
        return nc

    pa2.close()
    pb = ExitStack()
    ps_ya = pb.enter_context(nc.psum_tensor("ps_ya", [128, 1024], F32))
    ps_yb = pb.enter_context(nc.psum_tensor("ps_yb", [128, 1024], F32))

    k.wait("tensor", s_gp, GP_PROWS)
    k.do("tensor", "matmul", ps_sa[:, 0:192], onesf[:], prows[0:1, 0:192],
         start=True, stop=True, inc=[(s_mm, 1)])
    k.wait("vector", s_mm)
    k.do("vector", "tensor_copy", prep[:, :], ps_sa[:, 0:192], inc=[(s_ve, 1)])

    rg_rdy = {}
    ve_comb = {}
    for h in range(NH):
        slot = (h % 2) * 3 * HP
        if h >= 2:
            k.wait("gpsimd", s_ve, ve_comb[h - 2])
        for j in range(3):
            k.do("gpsimd", "dma_start",
                 out=rg[0:64, slot + j * HP:slot + (j + 1) * HP],
                 in_=cc_dst.ap()[j * 2049 + h * 64:j * 2049 + (h + 1) * 64, :],
                 inc=[(s_rg[h % 2], 16)])
        rg_rdy[h] = s_rg[h % 2].n
        r0 = rg[0:64, slot + 0 * HP:slot + 1 * HP]
        r1 = rg[0:64, slot + 1 * HP:slot + 2 * HP]
        r2 = rg[0:64, slot + 2 * HP:slot + 3 * HP]
        k.wait("vector", s_rg[h % 2], rg_rdy[h])
        k.do("vector", "scalar_tensor_tensor", i2scr[0:64, 0:HP], r0,
             prep[0:64, 64 + h:65 + h], r1, AL.mult, AL.add)
        k.do("vector", "scalar_tensor_tensor", i2scr[0:64, HP:2 * HP],
             i2scr[0:64, 0:HP], prep[0:64, 128 + h:129 + h], r2, AL.mult, AL.add)
        k.do("vector", "tensor_scalar", A(initb, h * HP, HP)[0:64, :], r0,
             aux[0:64, 65:66], None, AL.mult)
        k.do("vector", "scalar_tensor_tensor", A(initb, h * HP, HP)[0:64, :],
             i2scr[0:64, 0:HP], aux[0:64, 66:67], A(initb, h * HP, HP)[0:64, :],
             AL.mult, AL.add)
        k.do("vector", "scalar_tensor_tensor", A(initb, h * HP, HP)[0:64, :],
             i2scr[0:64, HP:2 * HP], aux[0:64, 67:68],
             A(initb, h * HP, HP)[0:64, :], AL.mult, AL.add, inc=[(s_ve, 1)])
        ve_comb[h] = s_ve.n
    VE_INIT = s_ve.n

    mt_guard = {0: 0, 1: 0}
    mt_ctr = 0
    for t in range(NT):
        for G in range(2):
            k.wait("gpsimd", s_ve, VE_INIT)
            k.wait("gpsimd", s_mm)
            k.do("gpsimd", "dma_start", out=csrow_hi[0:1, :],
                 in_=A(acsT_hi, t * CH, CH)[G * 16:G * 16 + 16, :],
                 inc=[(s_gp, 16)])
            k.do("gpsimd", "dma_start", out=csrow_lo[0:1, :],
                 in_=A(acsT_lo, t * CH, CH)[G * 16:G * 16 + 16, :],
                 inc=[(s_gp, 16)])
            GPF = s_gp.n
            k.wait("vector", s_ve, VE_INIT)
            for hh in range(16):
                h = G * 16 + hh
                if t == 0:
                    k.do("vector", "tensor_copy", A(prevt, hh * HP, HP)[0:64, :],
                         A(initb, h * HP, HP)[0:64, :])
                else:
                    k.do("vector", "scalar_tensor_tensor",
                         A(prevt, hh * HP, HP)[0:64, :],
                         A(initb, h * HP, HP)[0:64, :],
                         aux2[0:64, 256 + h * 8 + t:257 + h * 8 + t],
                         A(rloc, ((t - 1) * NH + h) * HP, HP)[0:64, :],
                         AL.mult, AL.add)
            k.do("vector", "engine_nop", inc=[(s_ve, 1)])
            PREV_G = s_ve.n

            for q4 in range(4):
                hb = q4 * 4
                pst = ps_sa if q4 % 2 == 0 else ps_sb
                k.wait("tensor", s_gp, GPF)
                k.wait("tensor", s_ve)
                k.do("tensor", "matmul", pst[:, 0:512], onesb[:],
                     csrow_hi[0:1, hb * 128:hb * 128 + 512],
                     start=True, stop=False)
                k.do("tensor", "matmul", pst[:, 0:512], onesb[:],
                     csrow_lo[0:1, hb * 128:hb * 128 + 512],
                     start=False, stop=True, inc=[(s_mm, 1)])
                MM_REP = s_mm.n
                for i4 in range(4):
                    hh = hb + i4
                    h = G * 16 + hh
                    slot = mt_ctr % 2
                    k.wait("vector", s_mm, MM_REP)
                    k.wait("vector", s_ac)
                    k.do("vector", "scalar_tensor_tensor",
                         t1f[:, slot * CH:slot * CH + CH],
                         pst[:, i4 * 128:i4 * 128 + 128],
                         A(nacs, t * NH + h, 1), m0c[:], AL.add, AL.add,
                         inc=[(s_ve, 1)])
                    k.wait("scalar", s_ve)
                    k.do("scalar", "activation",
                         t2f[:, slot * CH:slot * CH + CH],
                         t1f[:, slot * CH:slot * CH + CH], AF.Exp,
                         inc=[(s_ac, 1)])
                    k.wait("vector", s_ac)
                    k.wait("vector", s_mm, mt_guard[slot])
                    k.do("vector", "tensor_mul", A(mt, slot * CH, CH),
                         t2f[:, slot * CH:slot * CH + CH],
                         A(sct, t * CH, CH))
                    k.do("vector", "tensor_scalar", A(xdtd, slot * HP, HP),
                         A(xn, t * D + h * HP, HP), A(dtf, t * NH + h, 1), None,
                         AL.mult, inc=[(s_ve, 1)])
                    MT_RDY = s_ve.n
                    k.wait("tensor", s_ve, MT_RDY)
                    k.do("tensor", "matmul", ps_ya[:, hh * HP:hh * HP + HP],
                         A(mt, slot * CH, CH), A(xdtd, slot * HP, HP),
                         start=True, stop=True, inc=[(s_mm, 1)])
                    mt_guard[slot] = s_mm.n
                    mt_ctr += 1
            k.wait("tensor", s_ve, PREV_G)
            for hh in range(16):
                k.do("tensor", "matmul", ps_yb[:, hh * HP:hh * HP + HP],
                     A(cT, t * CH, CH)[0:64, :], A(prevt, hh * HP, HP)[0:64, :],
                     start=True, stop=True,
                     inc=[(s_mm, 1)] if hh == 15 else None)
            k.wait("vector", s_mm)
            for hh in range(16):
                h = G * 16 + hh
                k.do("vector", "tensor_scalar", yscr[:, 0:HP],
                     ps_yb[:, hh * HP:hh * HP + HP], A(sdo, t * NH + h, 1),
                     None, AL.mult)
                k.do("vector", "tensor_add", yscr[:, HP:2 * HP],
                     ps_ya[:, hh * HP:hh * HP + HP], yscr[:, 0:HP])
                k.do("vector", "scalar_tensor_tensor",
                     A(ybuf, t * D + h * HP, HP),
                     A(xn, t * D + h * HP, HP), aux[:, 32 + h:33 + h],
                     yscr[:, HP:2 * HP], AL.mult, AL.add, inc=[(s_ve, 1)])
            k.wait("tensor", s_ve)
    Y_DONE = s_ve.n

    if stop_after == "B":
        emit_dumps()
        pb.close(); pa.close(); ab.close(); ctx.close(); k.build()
        return nc

    ab.close()
    pb.close()
    pa.close()
    d2 = ExitStack()
    ytst = d2.enter_context(nc.sbuf_tensor("ytst", [128, DK * 256], BF16))
    owtb = d2.enter_context(nc.sbuf_tensor("owtb", [128, 4 * 1024], BF16))
    outbr = d2.enter_context(nc.sbuf_tensor("outbr", [1, D], BF16))
    pd = ExitStack()
    ps_u1 = pd.enter_context(nc.psum_tensor("ps_u1", [128, 128], BF16))
    ps_u2 = pd.enter_context(nc.psum_tensor("ps_u2", [128, 128], BF16))
    ps_fxd = pd.enter_context(nc.psum_tensor("ps_fxd", [128, 16], F32))
    ps_oa = pd.enter_context(nc.psum_tensor("ps_oa", [128, 1024], F32))
    ps_ob = pd.enter_context(nc.psum_tensor("ps_ob", [128, 1024], F32))

    if d_parts >= 1:
        k.do("gpsimd", "dma_start", out=outbr[:], in_=outb_hbm[:, :],
             inc=[(s_cldg, 16)])
    OUTB_LD = s_cldg.n

    nwo = 0
    tev0 = s_tev.n
    for tp in range(4 if d_parts >= 2 else 0):
        tA, tB = 2 * tp, 2 * tp + 1
        k.wait("tensor", s_ve, Y_DONE)
        k.wait("tensor", s_sev)
        nT2 = 0
        for dik in range(DK):
            for ti, tt in enumerate((tA, tB)):
                pst = ps_u1 if nT2 % 2 == 0 else ps_u2
                if nT2 >= 1:
                    k.wait("tensor", s_tev, tev0 + nT2)
                k.do("tensor", "transpose", pst[:, :],
                     A(ybuf, tt * D + dik * 128, 128), identb[:])
                k.do("tensor", "matmul", ps_fxd[0:1, 0:1], onesb[0:1, 0:1],
                     onesb[0:1, 0:1], start=True, stop=True, inc=[(s_tps, 1)])
                k.wait("vector", s_tps)
                k.do("vector", "tensor_copy",
                     A(ytst, dik * 256 + ti * 128, 128), pst[:, :],
                     inc=[(s_tev, 1)])
                nT2 += 1
        tev0 = s_tev.n
        YT_PAIR = s_tev.n
        if d_parts < 3:
            continue
        for dohalf in range(2):
            for dik in range(DK):
                if nwo >= 4:
                    k.wait("gpsimd", s_owt, nwo - 3)
                k.do("gpsimd", "dma_start",
                     out=A(owtb, (nwo % 4) * 1024, 1024),
                     in_=owt_hbm[dik * 128:(dik + 1) * 128,
                                 dohalf * 1024:(dohalf + 1) * 1024],
                     inc=[(s_ow[nwo % 4], 16)])
                k.wait("tensor", s_ow[nwo % 4], 16 * (nwo // 4 + 1))
                k.wait("tensor", s_tev, YT_PAIR)
                if dik == 0:
                    k.wait("tensor", s_cldg, OUTB_LD)
                    k.wait("tensor", s_sev)
                for ti in range(2):
                    pdst = (ps_oa, ps_ob)[ti]
                    for q in range(2):
                        k.do("tensor", "matmul", pdst[:, q * 512:(q + 1) * 512],
                             A(ytst, dik * 256 + ti * 128, 128),
                             A(owtb, (nwo % 4) * 1024 + q * 512, 512),
                             start=(dik == 0),
                             stop=(d_parts <= 3 and dik == DK - 1),
                             inc=[(s_owt, 1)] if (ti == 1 and q == 1) else None)
                nwo += 1
            for ti in range(2):
                pdst = (ps_oa, ps_ob)[ti]
                for q in range(2):
                    if d_parts >= 4:
                        k.do("tensor", "matmul", pdst[:, q * 512:(q + 1) * 512],
                             onesb[:],
                             outbr[0:1, dohalf * 1024 + q * 512:
                                   dohalf * 1024 + q * 512 + 512],
                             start=False, stop=True,
                             inc=[(s_mm, 1)] if (ti == 1 and q == 1) else None)
            if d_parts < 4:
                k.do("tensor", "nop", inc=[(s_mm, 1)])
            MM_S = s_mm.n
            k.wait("scalar", s_mm, MM_S)
            for ti, tt in enumerate((tA, tB)):
                if d_parts >= 4:
                    k.do("scalar", "copy",
                         A(ybuf, tt * D + dohalf * 1024, 512),
                         (ps_oa, ps_ob)[ti][:, 0:512])
                    k.do("scalar", "copy",
                         A(ybuf, tt * D + dohalf * 1024 + 512, 512),
                         (ps_oa, ps_ob)[ti][:, 512:1024], inc=[(s_sev, 1)])
                else:
                    k.do("scalar", "activation", spscr[0:1, 0:1] if False else A(ybuf, tt * D, 1),
                         A(ybuf, tt * D, 1), AF.Copy, inc=[(s_sev, 1)])
    S_DONE = s_sev.n

    if stop_after == "D":
        emit_dumps()
        pd.close(); d2.close(); ctx.close(); k.build()
        return nc

    d2.close()
    pd.close()
    c2 = ExitStack()
    sbc = lambda name, shape, dt: c2.enter_context(nc.sbuf_tensor(name, shape, dt))
    wgu = sbc("wgu", [128, 2 * DK * 512], BF16)
    hsp = sbc("hsp", [128, 64 * 512], BF16)
    hraw = sbc("hraw", [128, 4 * 512], BF16)
    keys = sbc("keys", [128, 6 * 256], F32)
    wdb = sbc("wdb", [128, 4 * 1024], BF16)
    gsc = sbc("gsc", [128, 2 * 1024], BF16)
    fscr = sbc("fscr", [128, 2 * 1024], F32)
    pc = ExitStack()
    ps_fa = pc.enter_context(nc.psum_tensor("ps_fa", [128, 1024], F32))
    ps_fb = pc.enter_context(nc.psum_tensor("ps_fb", [128, 1024], F32))
    ps_fc = pc.enter_context(nc.psum_tensor("ps_fc", [128, 1024], F32))
    ps_fd = pc.enter_context(nc.psum_tensor("ps_fd", [128, 1024], F32))

    WGU_FREE = {0: 0, 1: 0}
    GEV_GUARD = {}
    s_bar = Sem(nc, "dbar")
    k.do("tensor", "nop", inc=[(s_bar, 1)])
    k.do("vector", "engine_nop", inc=[(s_bar, 1)])
    k.do("scalar", "nop", inc=[(s_bar, 1)])
    k.do("gpsimd", "nop", inc=[(s_bar, 1)])
    k.do("sync", "nop", inc=[(s_bar, 1)])
    for e in ("tensor", "vector", "scalar", "gpsimd", "sync"):
        k.wait(e, s_bar, 5)
    k.wait("gpsimd", s_sev, S_DONE)
    k.wait("gpsimd", s_mm)
    WD_USED = {}
    FS_GUARD = {}
    nwd = 0
    for HF in range(2):
        tokc0 = HF * 512
        for ig in range(NIG):
            for w_i, (hbmw, off, swx) in enumerate(
                    ((wg_hbm, 0, s_wgl), (wu_hbm, DK * 512, s_wul))):
                k.wait("gpsimd", s_mm, WGU_FREE[w_i])
                for q in range(4):
                    k.do("gpsimd", "dma_start",
                         out=A(wgu, off, DK * 512).rearrange(
                             "p (k c) -> p k c", c=512)[:, q * 4:(q + 1) * 4, :],
                         in_=hbmw[:, ig * 512:(ig + 1) * 512].rearrange(
                             "(k p) c -> p k c", p=128)[:, q * 4:(q + 1) * 4, :],
                         inc=[(swx, 16)])
            WLD_G = s_wgl.n
            WLD_U = s_wul.n
            for phase in range(2):
                woff = phase * DK * 512
                pss = (ps_fa, ps_fb) if phase == 0 else (ps_fc, ps_fd)
                k.wait("tensor", s_wgl if phase == 0 else s_wul,
                       WLD_G if phase == 0 else WLD_U)
                if HF == 0 and ig == 0 and phase == 0:
                    k.wait("tensor", s_sev, S_DONE)
                if phase == 0 and (HF + ig) > 0:
                    k.wait("tensor", s_ac)
                if phase == 1 and (HF + ig) > 0:
                    k.wait("tensor", s_hev)
                for dk in range(DK):
                    for r in range(4):
                        pdst = pss[r // 2]
                        k.do("tensor", "matmul",
                             pdst[:, (r % 2) * 512:(r % 2) * 512 + 512],
                             A(wgu, woff + dk * 512, 512)[:, r::4],
                             A(xnt, dk * TOK + tokc0, 512),
                             start=(dk == 0), stop=(dk == DK - 1),
                             inc=[(s_mm, 1)] if (dk == DK - 1 and r == 3) else None)
                WGU_FREE[phase] = s_mm.n
                MMP = s_mm.n
                if phase == 0:
                    k.wait("scalar", s_mm, MMP)
                    k.wait("scalar", s_hev)
                    k.wait("scalar", s_spr)
                    for r in range(4):
                        k.do("scalar", "activation", A(hraw, r * 512, 512),
                             pss[r // 2][:, (r % 2) * 512:(r % 2) * 512 + 512],
                             AF.Silu, inc=[(s_ac, 1)] if r == 3 else None)
                else:
                    k.wait("vector", s_mm, MMP)
                    k.wait("vector", s_ac)
                    for r in range(4):
                        k.do("vector", "tensor_mul",
                             A(hsp, (ig * 4 + r) * 512, 512),
                             A(hraw, r * 512, 512),
                             pss[r // 2][:, (r % 2) * 512:(r % 2) * 512 + 512],
                             inc=[(s_hev, 1)])
            for half in range(2):
                co = half * 256
                k.wait("scalar", s_hev)
                k.wait("scalar", s_spr)
                for r in range(4):
                    k.do("scalar", "activation", A(keys, r * 256, 256),
                         A(hsp, (ig * 4 + r) * 512 + co, 256), AF.Abs,
                         scale=float(1.0 - r * NUDGE),
                         inc=[(s_ac, 1)] if r == 3 else None)
                k.wait("vector", s_ac)
                kk = [A(keys, r * 256, 256) for r in range(4)]
                thr = A(keys, 4 * 256, 256)
                k.do("vector", "tensor_max", thr, kk[0], kk[1])
                k.do("vector", "tensor_tensor", kk[0], kk[0], kk[1], AL.min)
                k.do("vector", "tensor_max", kk[1], kk[2], kk[3])
                k.do("vector", "tensor_tensor", kk[2], kk[2], kk[3], AL.min)
                k.do("vector", "tensor_tensor", kk[3], thr, kk[1], AL.min)
                k.do("vector", "tensor_max", kk[0], kk[0], kk[2])
                k.do("vector", "tensor_max", thr, kk[3], kk[0], inc=[(s_ve, 1)])
                k.wait("scalar", s_ve)
                for r in range(4):
                    k.do("scalar", "activation", A(keys, r * 256, 256),
                         A(hsp, (ig * 4 + r) * 512 + co, 256), AF.Abs,
                         scale=float(1.0 - r * NUDGE),
                         inc=[(s_ac, 1)] if r == 3 else None)
                k.wait("vector", s_ac)
                for r in range(4):
                    k.do("vector", "tensor_tensor", keys[:, 5 * 256:6 * 256],
                         kk[r], thr, AL.is_ge)
                    k.do("vector", "tensor_mul", A(hraw, r * 512, 256),
                         A(hsp, (ig * 4 + r) * 512 + co, 256),
                         keys[:, 5 * 256:6 * 256])
                    k.do("vector", "tensor_copy",
                         A(hsp, (ig * 4 + r) * 512 + co, 256),
                         A(hraw, r * 512, 256),
                         inc=[(s_spr, 1)] if r == 3 else None)
        SP_HALF = s_spr.n

        pslc = [(ps_fa, 0), (ps_fa, 512), (ps_fb, 0), (ps_fb, 512),
                (ps_fc, 0), (ps_fc, 512), (ps_fd, 0), (ps_fd, 512)]
        for dohalf in range(2):
            for tc in range(4):
                tglob = HF * 4 + tc
                key = (tc, (HF * 2 + dohalf) - 1)
                if key in GEV_GUARD:
                    k.wait("sync", s_gev, GEV_GUARD[key])
                k.do("sync", "dma_start",
                     out=xstage[:, tc * 1024:(tc + 1) * 1024],
                     in_=x_hbm[tglob * 128:(tglob + 1) * 128,
                               dohalf * 1024:(dohalf + 1) * 1024],
                     inc=[(s_xld2, 16)])
            XLD_H = s_xld2.n
            k.wait("tensor", s_spr, SP_HALF)
            k.wait("tensor", s_gev)
            k.wait("tensor", s_ac)
            k.wait("tensor", s_hev)
            for u in range(64):
                ig, r = u // 4, u % 4
                if nwd >= 4:
                    k.wait("gpsimd", s_mm, WD_USED[nwd - 4])
                k.do("gpsimd", "dma_start",
                     out=A(wdb, (nwd % 4) * 1024, 1024),
                     in_=wd_hbm[(ig * 4 + r) * 128:(ig * 4 + r) * 128 + 128,
                                dohalf * 1024:(dohalf + 1) * 1024],
                     inc=[(s_wd[nwd % 4], 16)])
                k.wait("tensor", s_wd[nwd % 4], 16 * (nwd // 4 + 1))
                for tc in range(4):
                    for q in range(2):
                        pdst, pc0 = pslc[tc * 2 + q]
                        k.do("tensor", "matmul", pdst[:, pc0:pc0 + 512],
                             A(hsp, (ig * 4 + r) * 512 + tc * 128, 128),
                             A(wdb, (nwd % 4) * 1024 + q * 512, 512),
                             start=(u == 0), stop=(u == 63),
                             inc=[(s_mm, 1)] if (tc == 3 and q == 1) else None)
                WD_USED[nwd] = s_mm.n
                nwd += 1
            MM_DOWN = s_mm.n
            k.wait("scalar", s_mm, MM_DOWN)
            for tc in range(4):
                tglob = HF * 4 + tc
                pdst = pslc[tc * 2][0]
                k.wait("scalar", s_gev)
                k.do("scalar", "activation", A(gsc, (tc % 2) * 1024, 1024),
                     pdst[:, 0:1024], AF.Silu, inc=[(s_ac, 1)])
                k.wait("vector", s_ac)
                k.wait("vector", s_xld2, XLD_H)
                k.wait("vector", s_st[tc % 2], FS_GUARD.get((tc % 2), 0))
                k.do("vector", "tensor_mul", A(fscr, (tc % 2) * 1024, 1024),
                     A(gsc, (tc % 2) * 1024, 1024),
                     A(ybuf, tglob * D + dohalf * 1024, 1024))
                k.do("vector", "tensor_add", A(fscr, (tc % 2) * 1024, 1024),
                     A(fscr, (tc % 2) * 1024, 1024),
                     xstage[:, tc * 1024:(tc + 1) * 1024],
                     inc=[(s_gev, 1)])
                GEV_GUARD[(tc, HF * 2 + dohalf)] = s_gev.n
                k.wait("scalar", s_gev)
                k.do("scalar", "dma_start",
                     out=out_hbm[tglob * 128:(tglob + 1) * 128,
                                 dohalf * 1024:(dohalf + 1) * 1024],
                     in_=A(fscr, (tc % 2) * 1024, 1024), inc=[(s_st[tc % 2], 16)])
                FS_GUARD[(tc % 2)] = s_st[tc % 2].n

    dbg_map.update(hsp=hsp, gsc=gsc, fscr=fscr, keys=keys, hraw=hraw)
    emit_dumps()
    k.wait("scalar", s_st[0], s_st[0].n)
    k.wait("scalar", s_st[1], s_st[1].n)
    k.wait("sync", s_st[0], s_st[0].n)
    k.wait("sync", s_st[1], s_st[1].n)

    pc.close(); c2.close(); ctx.close()
    k.build()
    return nc


def _layout_inputs(inputs, core):
    x = np.ascontiguousarray(np.asarray(inputs["L0_attn_x"]).reshape(8192, D))
    wg = np.ascontiguousarray(np.asarray(inputs["L0_ffn_w_gate"]))
    wu = np.ascontiguousarray(np.asarray(inputs["L0_ffn_w_up"]))
    wd = np.asarray(inputs["L0_ffn_w_down"])
    idx = np.empty(II, np.int64)
    idx.reshape(NIG, 4, 128)[:] = (np.arange(NIG)[:, None, None] * 512
                                   + np.arange(4)[None, :, None]
                                   + 4 * np.arange(128)[None, None, :])
    wd_perm = np.ascontiguousarray(wd[idx])
    ow_t = np.ascontiguousarray(np.asarray(inputs["L0_attn_out_w"]).T)
    bcdt = np.ascontiguousarray(np.concatenate(
        [np.asarray(inputs["L0_attn_B_w"]).T, np.asarray(inputs["L0_attn_C_w"]).T,
         np.asarray(inputs["L0_attn_dt_w"]).T], axis=1))
    bcdt_b = np.concatenate(
        [np.asarray(inputs["L0_attn_B_b"]), np.asarray(inputs["L0_attn_C_b"]),
         np.asarray(inputs["L0_attn_dt_b"])])[None, :].astype(np.float32)
    out_b = np.asarray(inputs["L0_attn_out_b"])[None, :].astype(np.float32)
    a_log = np.asarray(inputs["L0_attn_A_log"])[None, :].astype(np.float32)
    dvec = np.asarray(inputs["L0_attn_D"])[None, :].astype(np.float32)
    ident = np.eye(128, dtype=np.float32)
    iu = np.arange(128)
    m0 = np.where(iu[None, :] >= iu[:, None], 0.0, NEG).astype(np.float32)
    ones_row = np.ones((1, 128), np.float32)
    csel = np.zeros((1, 4), np.float32)
    csel[0, core % 4] = 1.0
    return {
        "x_own": np.ascontiguousarray(x[core * TOK:(core + 1) * TOK]),
        "wg": wg, "wu": wu, "wd_perm": wd_perm, "ow_t": ow_t,
        "bcdt_wt": bcdt, "bcdt_b": bcdt_b, "out_b": out_b,
        "a_log": a_log, "dvec": dvec, "ident128": ident, "m0": m0,
        "ones_row": ones_row, "csel": csel,
    }


def kernel(**inputs):
    nc = build_program()
    in_maps = [_layout_inputs(inputs, c) for c in range(8)]
    res = run_bass_kernel_spmd(nc, in_maps, core_ids=list(range(8)))
    out = np.concatenate([r["out"] for r in res.results], axis=0)
    return out.reshape(2, 4096, D)


# version 10
# speedup vs baseline: 1.3916x; 1.0125x over previous
import numpy as np
import ml_dtypes
from contextlib import ExitStack
import concourse.bass as bass
import concourse.mybir as mybir
from concourse.bass_utils import run_bass_kernel_spmd

F32 = mybir.dt.float32
BF16 = mybir.dt.bfloat16
AL = mybir.AluOpType
AF = mybir.ActivationFunctionType
BF = ml_dtypes.bfloat16

D = 2048
NH = 32
HP = 64
CH = 128
TOK = 1024
NT = 8
DK = 16
II = 8192
NIG = 16
EPS = 1e-5
NEG = -1e30
NUDGE = 2.0 ** -20


class Sem:
    def __init__(self, nc, name):
        self.h = nc.alloc_semaphore(name)
        self.n = 0


class K:
    ENGS = ("sync", "tensor", "vector", "scalar", "gpsimd")

    def __init__(self, nc):
        self.nc = nc
        self.ops = {e: [] for e in self.ENGS}

    def do(self, eng, fname, *args, inc=None, **kwargs):
        self.ops[eng].append(("op", fname, args, kwargs, inc))
        if inc is not None:
            for s, kk in inc:
                s.n += kk

    def wait(self, eng, sem, val=None):
        v = sem.n if val is None else val
        assert v <= sem.n, "wait target not yet produced at script time"
        self.ops[eng].append(("wait", sem, v))

    def _replay(self, eng_name, eng):
        for rec in self.ops[eng_name]:
            if rec[0] == "wait":
                _, sem, val = rec
                if val > 0:
                    eng.wait_ge(sem.h, val)
            else:
                _, fname, args, kwargs, inc = rec
                instr = getattr(eng, fname)(*args, **kwargs)
                if inc is not None:
                    for s, kk in inc:
                        instr = instr.then_inc(s.h, kk)

    def build(self):
        with self.nc.Block() as block:
            block.sync(lambda e: self._replay("sync", e))
            block.tensor(lambda e: self._replay("tensor", e))
            block.vector(lambda e: self._replay("vector", e))
            block.scalar(lambda e: self._replay("scalar", e))
            block.gpsimd(lambda e: self._replay("gpsimd", e))


def build_program(stop_after=None, detect_races=True, dbg=(), chunk_steps=99, d_parts=99):
    nc = bass.Bass("TRN2", target_bir_lowering=False,
                   dynamic_dma_scratch_size=16384,
                   detect_race_conditions=detect_races)

    def din(name, shape, dt=F32):
        return nc.dram_tensor(name, list(shape), dt, kind="ExternalInput").ap()

    x_hbm = din("x_own", [TOK, D])
    wg_hbm = din("wg", [D, II])
    wu_hbm = din("wu", [D, II])
    wd_hbm = din("wd_perm", [II, D])
    owt_hbm = din("ow_t", [D, D])
    bcdt_hbm = din("bcdt_wt", [D, 160])
    bcdtb_hbm = din("bcdt_b", [1, 160])
    outb_hbm = din("out_b", [1, D])
    alog_hbm = din("a_log", [1, NH])
    dvec_hbm = din("dvec", [1, NH])
    ident_hbm = din("ident128", [128, 128])
    m0_hbm = din("m0", [128, 128])
    ones_hbm = din("ones_row", [1, 128])
    csel_hbm = din("csel", [1, 4])
    out_hbm = nc.dram_tensor("out", [TOK, D], F32, kind="ExternalOutput").ap()

    cc_src = nc.dram_tensor("cc_src", [2049, 64], F32, kind="Internal")
    cc_dst = nc.dram_tensor("cc_dst", [4 * 2049, 64], F32, kind="Internal")

    ctx = ExitStack()
    sb = lambda name, shape, dt: ctx.enter_context(nc.sbuf_tensor(name, shape, dt))

    xnt = sb("xnt", [128, DK * TOK], BF16)
    ybuf = sb("ybuf", [128, NT * D], BF16)
    xstage = sb("xstage", [128, 2 * D], F32)
    identb = sb("identb", [128, 128], BF16)
    onesf = sb("onesf", [1, 128], F32)
    onesb = sb("onesb", [1, 128], BF16)

    k = K(nc)
    s_cld = Sem(nc, "cld")
    s_cldg = Sem(nc, "cldg")
    s_xld = [Sem(nc, "xld0"), Sem(nc, "xld1")]
    s_lnv = Sem(nc, "lnv")
    s_lna = Sem(nc, "lna")
    s_tps = Sem(nc, "tps")
    s_tev = Sem(nc, "tev")
    s_mm = Sem(nc, "smm")
    s_ve = Sem(nc, "sve")
    s_ac = Sem(nc, "sac")
    s_gp = Sem(nc, "sgp")
    s_rg = [Sem(nc, "srg0"), Sem(nc, "srg1")]
    s_cc = Sem(nc, "scc")
    s_wgl = Sem(nc, "wgl")
    s_wul = Sem(nc, "wul")
    s_wd = [Sem(nc, "swd%d" % i) for i in range(4)]
    s_ow = [Sem(nc, "sow%d" % i) for i in range(4)]
    s_st = [Sem(nc, "sst0"), Sem(nc, "sst1")]
    s_hev = Sem(nc, "hev")
    s_spr = Sem(nc, "spr")
    s_gev = Sem(nc, "gev")
    s_xld2 = Sem(nc, "xld2")
    s_owt = Sem(nc, "owt")
    s_sev = Sem(nc, "sev")

    def A(t, c0, w):
        return t[:, c0:c0 + w]

    ab = ExitStack()
    sba = lambda name, shape, dt: ab.enter_context(nc.sbuf_tensor(name, shape, dt))
    xn = sba("xn", [128, NT * D], BF16)
    identf = sba("identf", [128, 128], F32)
    m0c = sba("m0c", [128, 128], F32)
    aux = sba("aux", [128, 128], F32)
    aux2 = sba("aux2", [128, 512], F32)
    lncol = sba("lncol", [128, 48], F32)
    stat6 = sba("stat6", [128, 48], F32)
    rows = sba("rows", [1, 256], F32)
    bcdtw = sba("bcdtw", [128, DK * 160], BF16)
    bcdtb = sba("bcdtb", [1, 160], BF16)
    btok = sba("btok", [128, NT * HP], BF16)
    ctok = sba("ctok", [128, NT * HP], BF16)
    bT = sba("bT", [64, NT * CH], BF16)
    cT = sba("cT", [64, NT * CH], BF16)
    dtf = sba("dtf", [128, NT * NH], F32)
    dtds = sba("dtds", [128, NT * NH], F32)
    acs = sba("acs", [128, NT * NH], F32)
    nacs = sba("nacs", [128, NT * NH], F32)
    sdo = sba("sdo", [128, NT * NH], F32)
    acsT = sba("acsT", [32, CH], F32)
    acsT_hi = sba("acsT_hi", [32, NT * CH], BF16)
    acsT_lo = sba("acsT_lo", [32, NT * CH], BF16)
    onest = sba("onest", [32, CH], F32)
    alast = sba("alast", [32, NT], F32)
    pincl = sba("pincl", [32, NT], F32)
    pexcl = sba("pexcl", [32, NT], F32)
    flrow = sba("flrow", [1, 544], F32)
    csrow_hi = sba("csrow_hi", [1, 2048], BF16)
    csrow_lo = sba("csrow_lo", [1, 2048], BF16)
    sct = sba("sct", [128, NT * CH], BF16)
    spscr = sba("spscr", [128, 96], F32)
    datile = sba("datile", [128, NH], F32)
    rloc = sba("rloc", [64, NT * NH * HP], BF16)
    xdtp = sba("xdtp", [128, 2 * CH], BF16)
    xdtd = sba("xdtd", [128, 4 * HP], BF16)
    mt = sba("mt", [128, 4 * CH], BF16)
    t1f = sba("t1f", [128, 2 * CH], F32)
    t2f = sba("t2f", [128, 2 * CH], F32)
    yscr = sba("yscr", [128, 2 * HP], F32)
    rg = sba("rg", [64, 8 * HP], F32)
    prows = sba("prows", [1, 192], F32)
    prep = sba("prep", [128, 192], F32)
    initb = sba("initb", [64, NH * HP], F32)
    i2scr = sba("i2scr", [64, 2 * HP], F32)
    prevt = sba("prevt", [64, 16 * HP], BF16)

    dbg_map = dict(flrow=flrow, pincl=pincl, pexcl=pexcl, alast=alast,
               xn=xn, xnt=xnt, btok=btok, ctok=ctok, bT=bT, cT=cT,
                   dtf=dtf, dtds=dtds, acs=acs, nacs=nacs, sdo=sdo,
                   acsT_hi=acsT_hi, sct=sct, rloc=rloc,
                   aux=aux, aux2=aux2, prep=prep, initb=initb, ybuf=ybuf,
                   prevt=prevt, xstage=xstage)

    def emit_dumps():
        for name in dbg:
            if name in ("cc_src", "cc_dst"):
                t = {"cc_src": cc_src, "cc_dst": cc_dst}[name]
                dout = nc.dram_tensor("dbg_" + name, list(t.shape), F32,
                                      kind="ExternalOutput").ap()
                for s in (s_ve, s_ac, s_mm, s_gp):
                    k.wait("gpsimd", s)
                k.do("gpsimd", "dma_start", out=dout[:, :], in_=t.ap()[:, :],
                     inc=[(s_gp, 16)])
                k.wait("gpsimd", s_gp)
                continue
            t = dbg_map[name]
            shp = list(t.shape)
            dout = nc.dram_tensor("dbg_" + name, shp, F32,
                                  kind="ExternalOutput").ap()
            for s in (s_ve, s_ac, s_mm, s_tev, s_lna):
                k.wait("gpsimd", s)
            nrow = shp[0]
            nspl = max(1, (nrow * max(1, shp[1] // 8192)) // 512)
            k.do("gpsimd", "dma_start", out=dout[:, :], in_=t[:, :],
                 inc=[(s_gp, 16)])
            k.wait("gpsimd", s_gp)

    pa = ExitStack()
    ps_sa = pa.enter_context(nc.psum_tensor("ps_sa", [128, 512], F32))
    ps_sb = pa.enter_context(nc.psum_tensor("ps_sb", [128, 512], F32))
    ps_sd = pa.enter_context(nc.psum_tensor("ps_sd", [128, 512], F32))
    pa2 = ExitStack()
    ps_t1 = pa2.enter_context(nc.psum_tensor("ps_t1", [128, 128], BF16))
    ps_t2 = pa2.enter_context(nc.psum_tensor("ps_t2", [128, 128], BF16))
    ps_fx = pa2.enter_context(nc.psum_tensor("ps_fx", [128, 16], F32))

    k.do("gpsimd", "dma_start", out=identb[:], in_=ident_hbm[:, :], inc=[(s_cldg, 16)])
    k.do("sync", "dma_start", out=m0c[:], in_=m0_hbm[:, :], inc=[(s_cld, 16)])
    k.do("sync", "dma_start", out=identf[:], in_=ident_hbm[:, :], inc=[(s_cld, 16)])
    k.do("sync", "dma_start", out=onesf[:], in_=ones_hbm[:, :], inc=[(s_cld, 16)])
    k.do("sync", "dma_start", out=rows[0:1, 0:32], in_=alog_hbm[:, :], inc=[(s_cld, 16)])
    k.do("sync", "dma_start", out=rows[0:1, 32:64], in_=dvec_hbm[:, :], inc=[(s_cld, 16)])
    k.do("sync", "dma_start", out=rows[0:1, 64:68], in_=csel_hbm[:, :], inc=[(s_cld, 16)])
    k.do("gpsimd", "dma_start", out=onesb[:], in_=ones_hbm[:, :], inc=[(s_cldg, 16)])
    for qq in range(4):
        k.do("gpsimd", "dma_start",
             out=bcdtw[:].rearrange("p (k c) -> p k c", c=160)[:, qq * 4:(qq + 1) * 4, :],
             in_=bcdt_hbm[:, :].rearrange("(k p) c -> p k c", p=128)[:, qq * 4:(qq + 1) * 4, :],
             inc=[(s_cldg, 16)])
    k.do("gpsimd", "dma_start", out=bcdtb[:], in_=bcdtb_hbm[:, :], inc=[(s_cldg, 16)])
    CLD = s_cld.n
    CLDG = s_cldg.n

    k.wait("scalar", s_cld, CLD)
    k.wait("scalar", s_cldg, CLDG)
    k.do("scalar", "activation", rows[0:1, 128:160], rows[0:1, 0:32], AF.Exp,
         inc=[(s_ac, 1)])
    k.wait("tensor", s_ac)
    k.wait("tensor", s_cld, CLD)
    k.wait("tensor", s_cldg, CLDG)
    k.do("tensor", "matmul", ps_sa[:, 0:32], onesf[:], rows[0:1, 128:160],
         start=True, stop=True)
    k.do("tensor", "matmul", ps_sa[:, 32:64], onesf[:], rows[0:1, 32:64],
         start=True, stop=True)
    k.do("tensor", "matmul", ps_sa[:, 64:68], onesf[:], rows[0:1, 64:68],
         start=True, stop=True, inc=[(s_mm, 1)])
    k.wait("vector", s_mm)
    k.do("vector", "tensor_scalar", aux[:, 0:32], ps_sa[:, 0:32], -1.0, None, AL.mult)
    k.do("vector", "tensor_copy", aux[:, 32:68], ps_sa[:, 32:68])
    k.do("vector", "memset", onest[:], 1.0, inc=[(s_ve, 1)])
    VE_AUX = s_ve.n

    lna_tile = []
    for t in range(NT):
        sl = (t % 2) * D
        if t >= 2:
            k.wait("sync", s_lna, lna_tile[t - 2])
        k.do("sync", "dma_start", out=A(xstage, sl, D),
             in_=x_hbm[t * 128:(t + 1) * 128, :], inc=[(s_xld[t % 2], 16)])
        c = 6 * t
        st6 = stat6[:, (t % 2) * 24:(t % 2) * 24 + 24]
        k.wait("vector", s_xld[t % 2], 16 * (t // 2 + 1))
        for q in range(4):
            k.do("vector", "bn_stats", st6[:, q * 6:(q + 1) * 6],
                 A(xstage, sl + q * 512, 512),
                 inc=[(s_lnv, 1)] if q == 3 else None)
        k.wait("vector", s_lnv)
        k.do("vector", "bn_aggr", lncol[:, c:c + 2], st6, inc=[(s_lnv, 1)])
        k.wait("vector", s_lnv)
        k.do("vector", "tensor_scalar", lncol[:, c + 2:c + 3],
             lncol[:, c + 1:c + 2], float(EPS), None, AL.add, inc=[(s_lnv, 1)])
        k.wait("scalar", s_lnv)
        k.do("scalar", "activation", lncol[:, c + 3:c + 4], lncol[:, c + 2:c + 3],
             AF.Ln)
        k.do("scalar", "activation", lncol[:, c + 4:c + 5], lncol[:, c + 3:c + 4],
             AF.Exp, scale=-0.5, inc=[(s_lna, 1)])
        k.wait("vector", s_lna)
        k.do("vector", "scalar_tensor_tensor", lncol[:, c + 5:c + 6],
             lncol[:, c:c + 1], -1.0, lncol[:, c + 4:c + 5], AL.mult, AL.mult,
             inc=[(s_lnv, 1)])
        k.do("vector", "tensor_scalar", A(xn, t * D, D), A(xstage, sl, D),
             lncol[:, c + 4:c + 5], lncol[:, c + 5:c + 6], AL.mult, AL.add,
             inc=[(s_lna, 1)])
        lna_tile.append(s_lna.n)

    if stop_after == "A1":
        pa2.close(); pa.close(); ab.close(); ctx.close(); k.build()
        return nc

    k.wait("tensor", s_cld, CLD)
    k.wait("tensor", s_cldg, CLDG)
    nT = 0
    for t in range(NT):
        k.wait("tensor", s_lna, lna_tile[t])
        for dk in range(DK):
            pst = ps_t1 if nT % 2 == 0 else ps_t2
            if nT >= 2:
                k.wait("tensor", s_tev, nT - 1)
            k.do("tensor", "transpose", pst[:, :],
                 A(xn, t * D + dk * 128, 128), identb[:])
            k.do("tensor", "matmul", ps_fx[0:1, 0:1], onesb[0:1, 0:1],
                 onesb[0:1, 0:1], start=True, stop=True, inc=[(s_tps, 1)])
            k.wait("vector", s_tps)
            k.do("vector", "tensor_copy", A(xnt, dk * TOK + t * 128, 128),
                 pst[:, :], inc=[(s_tev, 1)])
            nT += 1

    if stop_after == "A2":
        pa2.close(); pa.close(); ab.close(); ctx.close(); k.build()
        return nc

    for t in range(NT):
        k.wait("tensor", s_tev, (t + 1) * DK)
        if t > 0:
            k.wait("tensor", s_ac)
            k.wait("tensor", s_ve)
        for dk in range(DK):
            k.do("tensor", "matmul", ps_sb[:, 0:160],
                 A(xnt, dk * TOK + t * 128, 128), A(bcdtw, dk * 160, 160),
                 start=(dk == 0), stop=False)
        k.do("tensor", "matmul", ps_sb[:, 0:160], onesb[:], bcdtb[:],
             start=False, stop=True, inc=[(s_mm, 1)])
        MM_PROJ = s_mm.n
        k.wait("vector", s_mm, MM_PROJ)
        k.do("vector", "tensor_copy", A(btok, t * HP, HP), ps_sb[:, 0:64])
        k.do("vector", "tensor_copy", A(ctok, t * HP, HP), ps_sb[:, 64:128],
             inc=[(s_ve, 1)])
        if chunk_steps < 2:
            continue
        k.wait("scalar", s_mm, MM_PROJ)
        k.wait("scalar", s_ve)
        k.do("scalar", "activation", spscr[:, 0:32], ps_sb[:, 128:160], AF.Abs)
        k.do("scalar", "activation", spscr[:, 32:64], spscr[:, 0:32], AF.Exp,
             scale=-1.0)
        k.do("scalar", "activation", spscr[:, 64:96], spscr[:, 32:64], AF.Ln,
             bias=1.0)
        k.do("scalar", "activation", spscr[:, 0:32], ps_sb[:, 128:160], AF.Relu,
             inc=[(s_ac, 1)])
        k.wait("vector", s_ac)
        k.do("vector", "tensor_add", A(dtf, t * NH, NH), spscr[:, 0:32],
             spscr[:, 64:96])
        k.wait("vector", s_ve, VE_AUX)
        k.do("vector", "tensor_mul", datile[:], A(dtf, t * NH, NH), aux[:, 0:32],
             inc=[(s_ve, 1)])
        if chunk_steps < 3:
            continue
        k.wait("tensor", s_ve)
        k.do("tensor", "transpose", ps_sa[0:32, 128:256],
             datile[:], identf[:])
        k.do("tensor", "matmul", ps_fx[0:1, 0:1], onesb[0:1, 0:1],
             onesb[0:1, 0:1], start=True, stop=True, inc=[(s_tps, 1)])
        k.wait("vector", s_tps)
        k.do("vector", "tensor_tensor_scan", acsT[:, :], onest[:],
             ps_sa[0:32, 128:256], 0.0, AL.mult, AL.add, inc=[(s_ve, 1)])
        k.wait("vector", s_ve)
        k.wait("tensor", s_ve)
        k.do("tensor", "transpose", ps_sa[:, 256:288], acsT[:, :],
             identf[0:32, 0:32])
        k.do("tensor", "matmul", ps_fx[0:1, 0:1], onesb[0:1, 0:1],
             onesb[0:1, 0:1], start=True, stop=True, inc=[(s_tps, 1)])
        k.wait("vector", s_tps)
        k.do("vector", "tensor_copy", A(acs, t * NH, NH), ps_sa[:, 256:288])
        k.do("vector", "tensor_scalar", A(nacs, t * NH, NH), ps_sa[:, 256:288],
             -1.0, None, AL.mult)
        k.do("vector", "tensor_copy", alast[:, t:t + 1], acsT[:, 127:128],
             inc=[(s_ve, 1)])
        if chunk_steps < 4:
            continue
        VE_ACS = s_ve.n
        k.wait("scalar", s_ve, VE_ACS)
        k.do("scalar", "activation", A(acsT_hi, t * CH, CH)[0:32, :], acsT[:, :],
             AF.Copy)
        k.do("scalar", "activation", A(sdo, t * NH, NH), A(acs, t * NH, NH),
             AF.Exp, inc=[(s_ac, 1)])
        k.wait("vector", s_ac)
        k.do("vector", "tensor_sub", A(acsT_lo, t * CH, CH)[0:32, :], acsT[:, :],
             A(acsT_hi, t * CH, CH)[0:32, :], inc=[(s_ve, 1)])
        if chunk_steps < 5:
            continue
        k.wait("gpsimd", s_ve, VE_ACS)
        k.wait("gpsimd", s_mm)
        k.do("gpsimd", "dma_start", out=flrow[0:1, 512:544],
             in_=alast[:, t:t + 1], inc=[(s_gp, 16)])
        k.wait("tensor", s_gp)
        k.wait("tensor", s_ve)
        k.do("tensor", "matmul", ps_sa[:, 288:320], onesf[:], flrow[0:1, 512:544],
             start=True, stop=True, inc=[(s_mm, 1)])
        k.wait("vector", s_mm)
        k.do("vector", "tensor_sub", t1f[:, 0:32], ps_sa[:, 288:320],
             A(acs, t * NH, NH), inc=[(s_ve, 1)])
        k.wait("scalar", s_ve)
        k.do("scalar", "activation", t2f[:, 0:32], t1f[:, 0:32], AF.Exp)
        k.do("scalar", "activation", t2f[:, 32:64], ps_sa[:, 288:320], AF.Exp,
             inc=[(s_ac, 1)])
        k.wait("vector", s_ac)
        k.do("vector", "tensor_mul", A(dtds, t * NH, NH), A(dtf, t * NH, NH),
             t2f[:, 0:32], inc=[(s_ve, 1)])
        VE_DTDS = s_ve.n

        if chunk_steps < 6:
            continue
        k.wait("tensor", s_ve, VE_DTDS)
        k.wait("tensor", s_tev)
        k.do("tensor", "transpose", ps_t1[0:64, :], A(btok, t * HP, HP),
             identb[:])
        k.do("tensor", "transpose", ps_t2[0:64, :], A(ctok, t * HP, HP),
             identb[:])
        k.do("tensor", "matmul", ps_fx[0:1, 0:1], onesb[0:1, 0:1],
             onesb[0:1, 0:1], start=True, stop=True, inc=[(s_tps, 1)])
        k.wait("vector", s_tps)
        k.do("vector", "tensor_copy", A(bT, t * CH, CH)[0:64, :], ps_t1[0:64, :])
        k.do("vector", "tensor_copy", A(cT, t * CH, CH)[0:64, :], ps_t2[0:64, :],
             inc=[(s_tev, 1)])
        k.wait("tensor", s_tev)
        k.do("tensor", "matmul", ps_sd[:, 0:128], A(bT, t * CH, CH)[0:64, :],
             A(cT, t * CH, CH)[0:64, :], start=True, stop=True, inc=[(s_mm, 1)])
        k.wait("vector", s_mm)
        k.do("vector", "tensor_copy", A(sct, t * CH, CH), ps_sd[:, 0:128],
             inc=[(s_ve, 1)])
        SCT_EV = s_ve.n

        if chunk_steps < 7:
            continue
        for hp2 in range(16):
            h0, h1 = 2 * hp2, 2 * hp2 + 1
            k.wait("vector", s_ve, VE_DTDS)
            k.do("vector", "tensor_scalar", A(xdtp, 0, HP),
                 A(xn, t * D + h0 * HP, HP), A(dtds, t * NH + h0, 1), None,
                 AL.mult)
            k.do("vector", "tensor_scalar", A(xdtp, HP, HP),
                 A(xn, t * D + h1 * HP, HP), A(dtds, t * NH + h1, 1), None,
                 AL.mult, inc=[(s_ve, 1)])
            k.wait("tensor", s_ve)
            pst, pc0 = (ps_sa, 320) if hp2 % 2 == 0 else (ps_sd, 256)
            k.do("tensor", "matmul", pst[0:64, pc0:pc0 + 128],
                 A(btok, t * HP, HP), xdtp[:, 0:128],
                 start=True, stop=True, inc=[(s_mm, 1)])
            k.wait("vector", s_mm)
            k.wait("vector", s_ac)
            for hh, hx in ((h0, 0), (h1, 1)):
                pcol = pc0 + hx * HP
                if t == 0:
                    k.do("vector", "tensor_copy",
                         A(rloc, (t * NH + hh) * HP, HP)[0:64, :],
                         pst[0:64, pcol:pcol + HP], inc=[(s_ve, 1)])
                else:
                    k.do("vector", "scalar_tensor_tensor",
                         A(rloc, (t * NH + hh) * HP, HP)[0:64, :],
                         A(rloc, ((t - 1) * NH + hh) * HP, HP)[0:64, :],
                         t2f[0:64, 32 + hh:33 + hh],
                         pst[0:64, pcol:pcol + HP], AL.mult, AL.add,
                         inc=[(s_ve, 1)])
    CHAIN_DONE = s_ve.n

    if stop_after == "A3":
        pa2.close(); pa.close(); ab.close(); ctx.close(); k.build()
        return nc

    k.do("vector", "tensor_tensor_scan", pincl[:, :], onest[:, 0:NT], alast[:, :],
         0.0, AL.mult, AL.add, inc=[(s_ve, 1)])
    k.wait("vector", s_ve)
    k.do("vector", "tensor_sub", pexcl[:, :], pincl[:, :], alast[:, :],
         inc=[(s_ve, 1)])
    k.wait("gpsimd", s_ve)
    k.do("gpsimd", "dma_start", out=flrow[0:1, 0:256], in_=alast[:, :],
         inc=[(s_gp, 16)])
    k.do("gpsimd", "dma_start", out=flrow[0:1, 256:512], in_=pexcl[:, :],
         inc=[(s_gp, 16)])
    k.wait("scalar", s_gp)
    k.wait("scalar", s_ve)
    k.do("scalar", "activation", flrow[0:1, 0:256], flrow[0:1, 0:256], AF.Exp)
    k.do("scalar", "activation", flrow[0:1, 256:512], flrow[0:1, 256:512], AF.Exp)
    k.do("scalar", "activation", spscr[0:32, 0:1], pincl[:, 7:8], AF.Exp,
         inc=[(s_ac, 1)])
    k.wait("tensor", s_ac)
    k.wait("tensor", s_ve)
    k.do("tensor", "matmul", ps_sa[:, 0:256], onesf[:], flrow[0:1, 0:256],
         start=True, stop=True)
    k.do("tensor", "matmul", ps_sa[:, 256:512], onesf[:], flrow[0:1, 256:512],
         start=True, stop=True, inc=[(s_mm, 1)])
    k.wait("vector", s_mm)
    k.do("vector", "tensor_copy", aux2[:, :], ps_sa[:, :], inc=[(s_ve, 1)])

    if stop_after == "A4":
        pa2.close(); pa.close(); ab.close(); ctx.close(); k.build()
        return nc

    k.wait("gpsimd", s_ve, CHAIN_DONE)
    k.wait("gpsimd", s_ac)
    for qq in range(4):
        k.do("gpsimd", "dma_start",
             out=cc_src.ap()[0:2048, :].rearrange("(h n) p -> n h p", n=64)[:, qq * 8:(qq + 1) * 8, :],
             in_=rloc[0:64, (NT - 1) * NH * HP:NT * NH * HP].rearrange(
                 "n (h p) -> n h p", p=HP)[:, qq * 8:(qq + 1) * 8, :],
             inc=[(s_gp, 16)])
    k.do("gpsimd", "dma_start", out=cc_src.ap()[2048:2049, 0:32],
         in_=spscr[0:32, 0:1], inc=[(s_gp, 16)])
    k.do("vector", "memset", spscr[0:32, 1:2], 0.0, inc=[(s_ve, 1)])
    k.wait("gpsimd", s_ve)
    k.do("gpsimd", "dma_start", out=cc_src.ap()[2048:2049, 32:64],
         in_=spscr[0:32, 1:2], inc=[(s_gp, 16)])
    k.wait("gpsimd", s_gp)
    k.do("gpsimd", "collective_compute", "AllGather", AL.bypass,
         replica_groups=[[0, 1, 2, 3], [4, 5, 6, 7]],
         ins=[cc_src.ap().opt()], outs=[cc_dst.ap().opt()], inc=[(s_cc, 1)])
    k.do("vector", "memset", prows[:], 0.0, inc=[(s_ve, 1)])
    PROWS_Z = s_ve.n
    k.wait("gpsimd", s_cc, 1)
    k.wait("gpsimd", s_ve, PROWS_Z)
    for j in range(3):
        k.do("gpsimd", "dma_start", out=prows[0:1, j * 64:j * 64 + 32],
             in_=cc_dst.ap()[j * 2049 + 2048:j * 2049 + 2049, 0:32],
             inc=[(s_gp, 16)])
    GP_PROWS = s_gp.n

    if stop_after == "A":
        emit_dumps()
        pa2.close(); pa.close(); ab.close(); ctx.close(); k.build()
        return nc

    pa2.close()
    pb = ExitStack()
    ps_ya = pb.enter_context(nc.psum_tensor("ps_ya", [128, 1024], F32))
    ps_yb = pb.enter_context(nc.psum_tensor("ps_yb", [128, 1024], F32))

    k.wait("tensor", s_gp, GP_PROWS)
    k.do("tensor", "matmul", ps_sa[:, 0:192], onesf[:], prows[0:1, 0:192],
         start=True, stop=True, inc=[(s_mm, 1)])
    k.wait("vector", s_mm)
    k.do("vector", "tensor_copy", prep[:, :], ps_sa[:, 0:192], inc=[(s_ve, 1)])

    rg_rdy = {}
    ve_comb = {}
    for h in range(NH):
        slot = (h % 2) * 3 * HP
        if h >= 2:
            k.wait("gpsimd", s_ve, ve_comb[h - 2])
        for j in range(3):
            k.do("gpsimd", "dma_start",
                 out=rg[0:64, slot + j * HP:slot + (j + 1) * HP],
                 in_=cc_dst.ap()[j * 2049 + h * 64:j * 2049 + (h + 1) * 64, :],
                 inc=[(s_rg[h % 2], 16)])
        rg_rdy[h] = s_rg[h % 2].n
        r0 = rg[0:64, slot + 0 * HP:slot + 1 * HP]
        r1 = rg[0:64, slot + 1 * HP:slot + 2 * HP]
        r2 = rg[0:64, slot + 2 * HP:slot + 3 * HP]
        k.wait("vector", s_rg[h % 2], rg_rdy[h])
        k.do("vector", "scalar_tensor_tensor", i2scr[0:64, 0:HP], r0,
             prep[0:64, 64 + h:65 + h], r1, AL.mult, AL.add)
        k.do("vector", "scalar_tensor_tensor", i2scr[0:64, HP:2 * HP],
             i2scr[0:64, 0:HP], prep[0:64, 128 + h:129 + h], r2, AL.mult, AL.add)
        k.do("vector", "tensor_scalar", A(initb, h * HP, HP)[0:64, :], r0,
             aux[0:64, 65:66], None, AL.mult)
        k.do("vector", "scalar_tensor_tensor", A(initb, h * HP, HP)[0:64, :],
             i2scr[0:64, 0:HP], aux[0:64, 66:67], A(initb, h * HP, HP)[0:64, :],
             AL.mult, AL.add)
        k.do("vector", "scalar_tensor_tensor", A(initb, h * HP, HP)[0:64, :],
             i2scr[0:64, HP:2 * HP], aux[0:64, 67:68],
             A(initb, h * HP, HP)[0:64, :], AL.mult, AL.add, inc=[(s_ve, 1)])
        ve_comb[h] = s_ve.n
    VE_INIT = s_ve.n

    mt_guard = {0: 0, 1: 0}
    mt_ctr = 0
    for t in range(NT):
        for G in range(2):
            k.wait("gpsimd", s_ve, VE_INIT)
            k.wait("gpsimd", s_mm)
            k.do("gpsimd", "dma_start", out=csrow_hi[0:1, :],
                 in_=A(acsT_hi, t * CH, CH)[G * 16:G * 16 + 16, :],
                 inc=[(s_gp, 16)])
            k.do("gpsimd", "dma_start", out=csrow_lo[0:1, :],
                 in_=A(acsT_lo, t * CH, CH)[G * 16:G * 16 + 16, :],
                 inc=[(s_gp, 16)])
            GPF = s_gp.n
            k.wait("vector", s_ve, VE_INIT)
            for hh in range(16):
                h = G * 16 + hh
                if t == 0:
                    k.do("vector", "tensor_copy", A(prevt, hh * HP, HP)[0:64, :],
                         A(initb, h * HP, HP)[0:64, :])
                else:
                    k.do("vector", "scalar_tensor_tensor",
                         A(prevt, hh * HP, HP)[0:64, :],
                         A(initb, h * HP, HP)[0:64, :],
                         aux2[0:64, 256 + h * 8 + t:257 + h * 8 + t],
                         A(rloc, ((t - 1) * NH + h) * HP, HP)[0:64, :],
                         AL.mult, AL.add)
            k.do("vector", "engine_nop", inc=[(s_ve, 1)])
            PREV_G = s_ve.n

            for q4 in range(4):
                hb = q4 * 4
                pst = ps_sa if q4 % 2 == 0 else ps_sb
                k.wait("tensor", s_gp, GPF)
                k.wait("tensor", s_ve)
                k.do("tensor", "matmul", pst[:, 0:512], onesb[:],
                     csrow_hi[0:1, hb * 128:hb * 128 + 512],
                     start=True, stop=False)
                k.do("tensor", "matmul", pst[:, 0:512], onesb[:],
                     csrow_lo[0:1, hb * 128:hb * 128 + 512],
                     start=False, stop=True, inc=[(s_mm, 1)])
                MM_REP = s_mm.n
                for i4 in range(4):
                    hh = hb + i4
                    h = G * 16 + hh
                    slot = mt_ctr % 2
                    k.wait("vector", s_mm, MM_REP)
                    k.wait("vector", s_ac)
                    k.do("vector", "scalar_tensor_tensor",
                         t1f[:, slot * CH:slot * CH + CH],
                         pst[:, i4 * 128:i4 * 128 + 128],
                         A(nacs, t * NH + h, 1), m0c[:], AL.add, AL.add,
                         inc=[(s_ve, 1)])
                    k.wait("scalar", s_ve)
                    k.do("scalar", "activation",
                         t2f[:, slot * CH:slot * CH + CH],
                         t1f[:, slot * CH:slot * CH + CH], AF.Exp,
                         inc=[(s_ac, 1)])
                    k.wait("vector", s_ac)
                    k.wait("vector", s_mm, mt_guard[slot])
                    k.do("vector", "tensor_mul", A(mt, slot * CH, CH),
                         t2f[:, slot * CH:slot * CH + CH],
                         A(sct, t * CH, CH))
                    k.do("vector", "tensor_scalar", A(xdtd, slot * HP, HP),
                         A(xn, t * D + h * HP, HP), A(dtf, t * NH + h, 1), None,
                         AL.mult, inc=[(s_ve, 1)])
                    MT_RDY = s_ve.n
                    k.wait("tensor", s_ve, MT_RDY)
                    k.do("tensor", "matmul", ps_ya[:, hh * HP:hh * HP + HP],
                         A(mt, slot * CH, CH), A(xdtd, slot * HP, HP),
                         start=True, stop=True, inc=[(s_mm, 1)])
                    mt_guard[slot] = s_mm.n
                    mt_ctr += 1
            k.wait("tensor", s_ve, PREV_G)
            for hh in range(16):
                k.do("tensor", "matmul", ps_yb[:, hh * HP:hh * HP + HP],
                     A(cT, t * CH, CH)[0:64, :], A(prevt, hh * HP, HP)[0:64, :],
                     start=True, stop=True,
                     inc=[(s_mm, 1)] if hh == 15 else None)
            k.wait("vector", s_mm)
            for hh in range(16):
                h = G * 16 + hh
                k.do("vector", "tensor_scalar", yscr[:, 0:HP],
                     ps_yb[:, hh * HP:hh * HP + HP], A(sdo, t * NH + h, 1),
                     None, AL.mult)
                k.do("vector", "tensor_add", yscr[:, HP:2 * HP],
                     ps_ya[:, hh * HP:hh * HP + HP], yscr[:, 0:HP])
                k.do("vector", "scalar_tensor_tensor",
                     A(ybuf, t * D + h * HP, HP),
                     A(xn, t * D + h * HP, HP), aux[:, 32 + h:33 + h],
                     yscr[:, HP:2 * HP], AL.mult, AL.add, inc=[(s_ve, 1)])
            k.wait("tensor", s_ve)
    Y_DONE = s_ve.n

    if stop_after == "B":
        emit_dumps()
        pb.close(); pa.close(); ab.close(); ctx.close(); k.build()
        return nc

    ab.close()
    pb.close()
    pa.close()
    d2 = ExitStack()
    ytst = d2.enter_context(nc.sbuf_tensor("ytst", [128, DK * 256], BF16))
    owtb = d2.enter_context(nc.sbuf_tensor("owtb", [128, 4 * 1024], BF16))
    outbr = d2.enter_context(nc.sbuf_tensor("outbr", [1, D], BF16))
    pd = ExitStack()
    ps_u1 = pd.enter_context(nc.psum_tensor("ps_u1", [128, 128], BF16))
    ps_u2 = pd.enter_context(nc.psum_tensor("ps_u2", [128, 128], BF16))
    ps_fxd = pd.enter_context(nc.psum_tensor("ps_fxd", [128, 16], F32))
    ps_oa = pd.enter_context(nc.psum_tensor("ps_oa", [128, 1024], F32))
    ps_ob = pd.enter_context(nc.psum_tensor("ps_ob", [128, 1024], F32))

    if d_parts >= 1:
        k.do("gpsimd", "dma_start", out=outbr[:], in_=outb_hbm[:, :],
             inc=[(s_cldg, 16)])
    OUTB_LD = s_cldg.n

    nwo = 0
    tev0 = s_tev.n
    for tp in range(4 if d_parts >= 2 else 0):
        tA, tB = 2 * tp, 2 * tp + 1
        k.wait("tensor", s_ve, Y_DONE)
        k.wait("tensor", s_sev)
        nT2 = 0
        for dik in range(DK):
            for ti, tt in enumerate((tA, tB)):
                pst = ps_u1 if nT2 % 2 == 0 else ps_u2
                if nT2 >= 1:
                    k.wait("tensor", s_tev, tev0 + nT2)
                k.do("tensor", "transpose", pst[:, :],
                     A(ybuf, tt * D + dik * 128, 128), identb[:])
                k.do("tensor", "matmul", ps_fxd[0:1, 0:1], onesb[0:1, 0:1],
                     onesb[0:1, 0:1], start=True, stop=True, inc=[(s_tps, 1)])
                k.wait("vector", s_tps)
                k.do("vector", "tensor_copy",
                     A(ytst, dik * 256 + ti * 128, 128), pst[:, :],
                     inc=[(s_tev, 1)])
                nT2 += 1
        tev0 = s_tev.n
        YT_PAIR = s_tev.n
        if d_parts < 3:
            continue
        for dohalf in range(2):
            for dik in range(DK):
                if nwo >= 4:
                    k.wait("gpsimd", s_owt, nwo - 3)
                k.do("gpsimd", "dma_start",
                     out=A(owtb, (nwo % 4) * 1024, 1024),
                     in_=owt_hbm[dik * 128:(dik + 1) * 128,
                                 dohalf * 1024:(dohalf + 1) * 1024],
                     inc=[(s_ow[nwo % 4], 16)])
                k.wait("tensor", s_ow[nwo % 4], 16 * (nwo // 4 + 1))
                k.wait("tensor", s_tev, YT_PAIR)
                if dik == 0:
                    k.wait("tensor", s_cldg, OUTB_LD)
                    k.wait("tensor", s_sev)
                for ti in range(2):
                    pdst = (ps_oa, ps_ob)[ti]
                    for q in range(2):
                        k.do("tensor", "matmul", pdst[:, q * 512:(q + 1) * 512],
                             A(ytst, dik * 256 + ti * 128, 128),
                             A(owtb, (nwo % 4) * 1024 + q * 512, 512),
                             start=(dik == 0),
                             stop=(d_parts <= 3 and dik == DK - 1),
                             inc=[(s_owt, 1)] if (ti == 1 and q == 1) else None)
                nwo += 1
            for ti in range(2):
                pdst = (ps_oa, ps_ob)[ti]
                for q in range(2):
                    if d_parts >= 4:
                        k.do("tensor", "matmul", pdst[:, q * 512:(q + 1) * 512],
                             onesb[:],
                             outbr[0:1, dohalf * 1024 + q * 512:
                                   dohalf * 1024 + q * 512 + 512],
                             start=False, stop=True,
                             inc=[(s_mm, 1)] if (ti == 1 and q == 1) else None)
            if d_parts < 4:
                k.do("tensor", "nop", inc=[(s_mm, 1)])
            MM_S = s_mm.n
            k.wait("scalar", s_mm, MM_S)
            for ti, tt in enumerate((tA, tB)):
                if d_parts >= 4:
                    k.do("scalar", "copy",
                         A(ybuf, tt * D + dohalf * 1024, 512),
                         (ps_oa, ps_ob)[ti][:, 0:512])
                    k.do("scalar", "copy",
                         A(ybuf, tt * D + dohalf * 1024 + 512, 512),
                         (ps_oa, ps_ob)[ti][:, 512:1024], inc=[(s_sev, 1)])
                else:
                    k.do("scalar", "activation", spscr[0:1, 0:1] if False else A(ybuf, tt * D, 1),
                         A(ybuf, tt * D, 1), AF.Copy, inc=[(s_sev, 1)])
    S_DONE = s_sev.n

    if stop_after == "D":
        emit_dumps()
        pd.close(); d2.close(); ctx.close(); k.build()
        return nc

    d2.close()
    pd.close()
    c2 = ExitStack()
    sbc = lambda name, shape, dt: c2.enter_context(nc.sbuf_tensor(name, shape, dt))
    wgu = sbc("wgu", [128, 2 * DK * 512], BF16)
    hsp = sbc("hsp", [128, 64 * 512], BF16)
    hraw = sbc("hraw", [128, 4 * 512], BF16)
    keys = sbc("keys", [128, 6 * 256], F32)
    wdb = sbc("wdb", [128, 4 * 1024], BF16)
    gsc = sbc("gsc", [128, 2 * 1024], BF16)
    fscr = sbc("fscr", [128, 2 * 1024], F32)
    pc = ExitStack()
    ps_fa = pc.enter_context(nc.psum_tensor("ps_fa", [128, 1024], F32))
    ps_fb = pc.enter_context(nc.psum_tensor("ps_fb", [128, 1024], F32))
    ps_fc = pc.enter_context(nc.psum_tensor("ps_fc", [128, 1024], F32))
    ps_fd = pc.enter_context(nc.psum_tensor("ps_fd", [128, 1024], F32))

    WGU_FREE = {0: 0, 1: 0}
    GEV_GUARD = {}
    s_bar = Sem(nc, "dbar")
    k.do("tensor", "nop", inc=[(s_bar, 1)])
    k.do("vector", "engine_nop", inc=[(s_bar, 1)])
    k.do("scalar", "nop", inc=[(s_bar, 1)])
    k.do("gpsimd", "nop", inc=[(s_bar, 1)])
    k.do("sync", "nop", inc=[(s_bar, 1)])
    for e in ("tensor", "vector", "scalar", "gpsimd", "sync"):
        k.wait(e, s_bar, 5)
    k.wait("gpsimd", s_sev, S_DONE)
    k.wait("gpsimd", s_mm)
    WD_USED = {}
    FS_GUARD = {}
    nwd = 0
    s_wq = [Sem(nc, f"wq{i}") for i in range(8)]
    WQ_CNT = {}
    WGU_FREE_Q = {}
    for HF in range(2):
        tokc0 = HF * 512
        for ig in range(NIG):
            for w_i, (hbmw, off) in enumerate(
                    ((wg_hbm, 0), (wu_hbm, DK * 512))):
                for q in range(4):
                    k.wait("gpsimd", s_mm, WGU_FREE_Q.get((w_i, q), 0))
                    k.do("gpsimd", "dma_start",
                         out=A(wgu, off, DK * 512).rearrange(
                             "p (k c) -> p k c", c=512)[:, q * 4:(q + 1) * 4, :],
                         in_=hbmw[:, ig * 512:(ig + 1) * 512].rearrange(
                             "(k p) c -> p k c", p=128)[:, q * 4:(q + 1) * 4, :],
                         inc=[(s_wq[w_i * 4 + q], 16)])
                    WQ_CNT[(w_i, q)] = WQ_CNT.get((w_i, q), 0) + 16
            for phase in range(2):
                woff = phase * DK * 512
                pss = (ps_fa, ps_fb) if phase == 0 else (ps_fc, ps_fd)
                if HF == 0 and ig == 0 and phase == 0:
                    k.wait("tensor", s_sev, S_DONE)
                if phase == 0 and (HF + ig) > 0:
                    k.wait("tensor", s_ac)
                if phase == 1 and (HF + ig) > 0:
                    k.wait("tensor", s_hev)
                for dk in range(DK):
                    if dk % 4 == 0:
                        k.wait("tensor", s_wq[phase * 4 + dk // 4],
                               WQ_CNT[(phase, dk // 4)])
                    for r in range(4):
                        pdst = pss[r // 2]
                        k.do("tensor", "matmul",
                             pdst[:, (r % 2) * 512:(r % 2) * 512 + 512],
                             A(wgu, woff + dk * 512, 512)[:, r::4],
                             A(xnt, dk * TOK + tokc0, 512),
                             start=(dk == 0), stop=(dk == DK - 1),
                             inc=[(s_mm, 1)] if (dk % 4 == 3 and r == 3) else None)
                    if dk % 4 == 3:
                        WGU_FREE_Q[(phase, dk // 4)] = s_mm.n
                MMP = s_mm.n
                if phase == 0:
                    k.wait("scalar", s_mm, MMP)
                    k.wait("scalar", s_hev)
                    k.wait("scalar", s_spr)
                    for r in range(4):
                        k.do("scalar", "activation", A(hraw, r * 512, 512),
                             pss[r // 2][:, (r % 2) * 512:(r % 2) * 512 + 512],
                             AF.Silu, inc=[(s_ac, 1)] if r == 3 else None)
                else:
                    k.wait("vector", s_mm, MMP)
                    k.wait("vector", s_ac)
                    for r in range(4):
                        k.do("vector", "tensor_mul",
                             A(hsp, (ig * 4 + r) * 512, 512),
                             A(hraw, r * 512, 512),
                             pss[r // 2][:, (r % 2) * 512:(r % 2) * 512 + 512],
                             inc=[(s_hev, 1)])
            for half in range(2):
                co = half * 256
                k.wait("scalar", s_hev)
                k.wait("scalar", s_spr)
                for r in range(4):
                    k.do("scalar", "activation", A(keys, r * 256, 256),
                         A(hsp, (ig * 4 + r) * 512 + co, 256), AF.Abs,
                         scale=float(1.0 - r * NUDGE),
                         inc=[(s_ac, 1)] if r == 3 else None)
                k.wait("vector", s_ac)
                kk = [A(keys, r * 256, 256) for r in range(4)]
                thr = A(keys, 4 * 256, 256)
                k.do("vector", "tensor_max", thr, kk[0], kk[1])
                k.do("vector", "tensor_tensor", kk[0], kk[0], kk[1], AL.min)
                k.do("vector", "tensor_max", kk[1], kk[2], kk[3])
                k.do("vector", "tensor_tensor", kk[2], kk[2], kk[3], AL.min)
                k.do("vector", "tensor_tensor", kk[3], thr, kk[1], AL.min)
                k.do("vector", "tensor_max", kk[0], kk[0], kk[2])
                k.do("vector", "tensor_max", thr, kk[3], kk[0], inc=[(s_ve, 1)])
                k.wait("scalar", s_ve)
                for r in range(4):
                    k.do("scalar", "activation", A(keys, r * 256, 256),
                         A(hsp, (ig * 4 + r) * 512 + co, 256), AF.Abs,
                         scale=float(1.0 - r * NUDGE),
                         inc=[(s_ac, 1)] if r == 3 else None)
                k.wait("vector", s_ac)
                for r in range(4):
                    k.do("vector", "tensor_tensor", keys[:, 5 * 256:6 * 256],
                         kk[r], thr, AL.is_ge)
                    k.do("vector", "tensor_mul", A(hraw, r * 512, 256),
                         A(hsp, (ig * 4 + r) * 512 + co, 256),
                         keys[:, 5 * 256:6 * 256])
                    k.do("vector", "tensor_copy",
                         A(hsp, (ig * 4 + r) * 512 + co, 256),
                         A(hraw, r * 512, 256),
                         inc=[(s_spr, 1)] if r == 3 else None)
        SP_HALF = s_spr.n

        pslc = [(ps_fa, 0), (ps_fa, 512), (ps_fb, 0), (ps_fb, 512),
                (ps_fc, 0), (ps_fc, 512), (ps_fd, 0), (ps_fd, 512)]
        for dohalf in range(2):
            for tc in range(4):
                tglob = HF * 4 + tc
                key = (tc, (HF * 2 + dohalf) - 1)
                if key in GEV_GUARD:
                    k.wait("sync", s_gev, GEV_GUARD[key])
                k.do("sync", "dma_start",
                     out=xstage[:, tc * 1024:(tc + 1) * 1024],
                     in_=x_hbm[tglob * 128:(tglob + 1) * 128,
                               dohalf * 1024:(dohalf + 1) * 1024],
                     inc=[(s_xld2, 16)])
            XLD_H = s_xld2.n
            k.wait("tensor", s_spr, SP_HALF)
            k.wait("tensor", s_gev)
            k.wait("tensor", s_ac)
            k.wait("tensor", s_hev)
            for u in range(64):
                ig, r = u // 4, u % 4
                if nwd >= 4:
                    k.wait("gpsimd", s_mm, WD_USED[nwd - 4])
                k.do("gpsimd", "dma_start",
                     out=A(wdb, (nwd % 4) * 1024, 1024),
                     in_=wd_hbm[(ig * 4 + r) * 128:(ig * 4 + r) * 128 + 128,
                                dohalf * 1024:(dohalf + 1) * 1024],
                     inc=[(s_wd[nwd % 4], 16)])
                k.wait("tensor", s_wd[nwd % 4], 16 * (nwd // 4 + 1))
                for tc in range(4):
                    for q in range(2):
                        pdst, pc0 = pslc[tc * 2 + q]
                        k.do("tensor", "matmul", pdst[:, pc0:pc0 + 512],
                             A(hsp, (ig * 4 + r) * 512 + tc * 128, 128),
                             A(wdb, (nwd % 4) * 1024 + q * 512, 512),
                             start=(u == 0), stop=(u == 63),
                             inc=[(s_mm, 1)] if (tc == 3 and q == 1) else None)
                WD_USED[nwd] = s_mm.n
                nwd += 1
            MM_DOWN = s_mm.n
            k.wait("scalar", s_mm, MM_DOWN)
            for tc in range(4):
                tglob = HF * 4 + tc
                pdst = pslc[tc * 2][0]
                k.wait("scalar", s_gev)
                k.do("scalar", "activation", A(gsc, (tc % 2) * 1024, 1024),
                     pdst[:, 0:1024], AF.Silu, inc=[(s_ac, 1)])
                k.wait("vector", s_ac)
                k.wait("vector", s_xld2, XLD_H)
                k.wait("vector", s_st[tc % 2], FS_GUARD.get((tc % 2), 0))
                k.do("vector", "tensor_mul", A(fscr, (tc % 2) * 1024, 1024),
                     A(gsc, (tc % 2) * 1024, 1024),
                     A(ybuf, tglob * D + dohalf * 1024, 1024))
                k.do("vector", "tensor_add", A(fscr, (tc % 2) * 1024, 1024),
                     A(fscr, (tc % 2) * 1024, 1024),
                     xstage[:, tc * 1024:(tc + 1) * 1024],
                     inc=[(s_gev, 1)])
                GEV_GUARD[(tc, HF * 2 + dohalf)] = s_gev.n
                k.wait("scalar", s_gev)
                k.do("scalar", "dma_start",
                     out=out_hbm[tglob * 128:(tglob + 1) * 128,
                                 dohalf * 1024:(dohalf + 1) * 1024],
                     in_=A(fscr, (tc % 2) * 1024, 1024), inc=[(s_st[tc % 2], 16)])
                FS_GUARD[(tc % 2)] = s_st[tc % 2].n

    dbg_map.update(hsp=hsp, gsc=gsc, fscr=fscr, keys=keys, hraw=hraw)
    emit_dumps()
    k.wait("scalar", s_st[0], s_st[0].n)
    k.wait("scalar", s_st[1], s_st[1].n)
    k.wait("sync", s_st[0], s_st[0].n)
    k.wait("sync", s_st[1], s_st[1].n)

    pc.close(); c2.close(); ctx.close()
    k.build()
    return nc


def _layout_inputs(inputs, core):
    x = np.ascontiguousarray(np.asarray(inputs["L0_attn_x"]).reshape(8192, D))
    wg = np.ascontiguousarray(np.asarray(inputs["L0_ffn_w_gate"]))
    wu = np.ascontiguousarray(np.asarray(inputs["L0_ffn_w_up"]))
    wd = np.asarray(inputs["L0_ffn_w_down"])
    idx = np.empty(II, np.int64)
    idx.reshape(NIG, 4, 128)[:] = (np.arange(NIG)[:, None, None] * 512
                                   + np.arange(4)[None, :, None]
                                   + 4 * np.arange(128)[None, None, :])
    wd_perm = np.ascontiguousarray(wd[idx])
    ow_t = np.ascontiguousarray(np.asarray(inputs["L0_attn_out_w"]).T)
    bcdt = np.ascontiguousarray(np.concatenate(
        [np.asarray(inputs["L0_attn_B_w"]).T, np.asarray(inputs["L0_attn_C_w"]).T,
         np.asarray(inputs["L0_attn_dt_w"]).T], axis=1))
    bcdt_b = np.concatenate(
        [np.asarray(inputs["L0_attn_B_b"]), np.asarray(inputs["L0_attn_C_b"]),
         np.asarray(inputs["L0_attn_dt_b"])])[None, :].astype(np.float32)
    out_b = np.asarray(inputs["L0_attn_out_b"])[None, :].astype(np.float32)
    a_log = np.asarray(inputs["L0_attn_A_log"])[None, :].astype(np.float32)
    dvec = np.asarray(inputs["L0_attn_D"])[None, :].astype(np.float32)
    ident = np.eye(128, dtype=np.float32)
    iu = np.arange(128)
    m0 = np.where(iu[None, :] >= iu[:, None], 0.0, NEG).astype(np.float32)
    ones_row = np.ones((1, 128), np.float32)
    csel = np.zeros((1, 4), np.float32)
    csel[0, core % 4] = 1.0
    return {
        "x_own": np.ascontiguousarray(x[core * TOK:(core + 1) * TOK]),
        "wg": wg, "wu": wu, "wd_perm": wd_perm, "ow_t": ow_t,
        "bcdt_wt": bcdt, "bcdt_b": bcdt_b, "out_b": out_b,
        "a_log": a_log, "dvec": dvec, "ident128": ident, "m0": m0,
        "ones_row": ones_row, "csel": csel,
    }


def kernel(**inputs):
    nc = build_program()
    in_maps = [_layout_inputs(inputs, c) for c in range(8)]
    res = run_bass_kernel_spmd(nc, in_maps, core_ids=list(range(8)))
    out = np.concatenate([r["out"] for r in res.results], axis=0)
    return out.reshape(2, 4096, D)
